# version 12
# baseline (speedup 1.0000x reference)
import os
import numpy as np
from contextlib import ExitStack

import concourse.bass as bass
import concourse.mybir as mybir
from concourse.bass_utils import run_bass_kernel_spmd

MK_PH = int(os.environ.get("MK_PH", "99"))

F32 = mybir.dt.float32
BF16 = mybir.dt.bfloat16
AF = mybir.ActivationFunctionType
ALU = mybir.AluOpType
AX = mybir.AxisListType

D = 2048
S_ALL = 4096
S_OWN = 2048
H = 5632
NT = 32
NTO = 16
NS = 16
NH = 44
CH = 1024
NK = 2
NW = float(H * D)


class Sem:
    def __init__(self, h):
        self.h = h
        self.n = 0


class Rec:
    ENGS = ("sync", "scalar", "vector", "tensor", "gpsimd")

    def __init__(self):
        self.ops = {k: [] for k in self.ENGS}

    def emit(self, eng, f):
        self.ops[eng].append(f)

    def wait(self, eng, sem, v=None):
        v = sem.n if v is None else v
        assert 0 <= v <= sem.n, f"wait beyond emitted count: {v} > {sem.n}"
        if v > 0:
            self.emit(eng, lambda e, s=sem.h, vv=v: e.wait_ge(s, vv))

    def op(self, eng, f, inc=None, k=1):
        if inc is None:
            self.emit(eng, f)
            return None
        inc.n += k
        self.emit(eng, lambda e, ff=f, s=inc.h, kk=k: ff(e).then_inc(s, kk))
        return inc.n


class DmaChan:

    def __init__(self, rec, nc, es, name, nslots):
        self.r = rec
        self.sems = [Sem(es.enter_context(nc.semaphore(f"{name}{i}")))
                     for i in range(nslots)]

    def dma(self, eng, slot, f):
        return self.r.op(eng, f, inc=self.sems[slot], k=16)

    def wait(self, eng, slot, v=None):
        self.r.wait(eng, self.sems[slot], v)

    def wait_all(self, eng):
        for s in self.sems:
            self.r.wait(eng, s)


def build_program():
    nc = bass.Bass("TRN2", target_bir_lowering=False)

    xb = nc.dram_tensor("xb", [S_ALL, D], F32, kind="ExternalInput")
    xtb = nc.dram_tensor("xtb", [D, S_OWN], F32, kind="ExternalInput")
    w1t = nc.dram_tensor("w1t", [NH, 128, 16, 128], F32, kind="ExternalInput")
    w2t = nc.dram_tensor("w2t", [NH, 128, 16, 128], F32, kind="ExternalInput")
    w3t = nc.dram_tensor("w3t", [NS, 128, NH, 128], F32, kind="ExternalInput")
    outT = nc.dram_tensor("outT", [D, S_OWN], F32, kind="ExternalOutput")

    xtd = nc.dram_tensor("xtd", [NS, 128, S_OWN], BF16, kind="Internal")
    pd = nc.dram_tensor("pd", [NS, 16, 128, 128], BF16, kind="Internal")
    sTd = nc.dram_tensor("sTd", [NS, 128, S_OWN], F32, kind="Internal")
    qd1 = nc.dram_tensor("qd1", [NH, 128, 16, 128], BF16, kind="Internal")
    qd2 = nc.dram_tensor("qd2", [NH, 128, 16, 128], BF16, kind="Internal")
    qd3 = nc.dram_tensor("qd3", [NS, 128, NH, 128], BF16, kind="Internal")

    es = ExitStack()
    mksem = lambda n: Sem(es.enter_context(nc.semaphore(n)))

    wg = es.enter_context(nc.sbuf_tensor("wg", [128, 4, 2048], F32))
    sums = es.enter_context(nc.sbuf_tensor("sums", [128, NT], F32))
    sqs = es.enter_context(nc.sbuf_tensor("sqs", [128, NT], F32))
    meanc = es.enter_context(nc.sbuf_tensor("meanc", [128, NT], F32))
    msqc = es.enter_context(nc.sbuf_tensor("msqc", [128, NT], F32))
    m2c = es.enter_context(nc.sbuf_tensor("m2c", [128, NT], F32))
    vepsc = es.enter_context(nc.sbuf_tensor("vepsc", [128, NT], F32))
    stdc = es.enter_context(nc.sbuf_tensor("stdc", [128, NT], F32))
    rstdc = es.enter_context(nc.sbuf_tensor("rstdc", [128, NT], F32))
    zcol = es.enter_context(nc.sbuf_tensor("zcol", [128, NS], F32))
    rzcol = es.enter_context(nc.sbuf_tensor("rzcol", [128, NS], F32))
    dcol = es.enter_context(nc.sbuf_tensor("dcol", [128, NS], F32))
    ndcol = es.enter_context(nc.sbuf_tensor("ndcol", [128, NS], F32))
    gcols = es.enter_context(nc.sbuf_tensor("gcols", [128, 2 * NH + 3 * NS], F32))
    gsum3 = es.enter_context(nc.sbuf_tensor("gsum3", [128, 4], F32))
    grep3 = es.enter_context(nc.sbuf_tensor("grep3", [128, 4], F32))
    t1g = es.enter_context(nc.sbuf_tensor("t1g", [128, 4], F32))
    tpos3 = es.enter_context(nc.sbuf_tensor("tpos3", [128, 4], F32))
    tneg3 = es.enter_context(nc.sbuf_tensor("tneg3", [128, 4], F32))
    onescolb = es.enter_context(nc.sbuf_tensor("onescolb", [128, 1], BF16))
    ones128f = es.enter_context(nc.sbuf_tensor("ones128f", [128, 128], F32))
    identb = es.enter_context(nc.sbuf_tensor("identb", [128, 128], BF16))
    sone = es.enter_context(nc.sbuf_tensor("sone", [128, 1], F32))
    shalf = es.enter_context(nc.sbuf_tensor("shalf", [128, 1], F32))
    squart = es.enter_context(nc.sbuf_tensor("squart", [128, 1], F32))
    g1dump = es.enter_context(nc.sbuf_tensor("g1dump", [128, 2048], F32))

    _dmp = {"i": 0, "marks": [None, None]}

    c_pe = mksem("c_pe")
    c_act = mksem("c_act")
    c_ve = mksem("c_ve")
    c_po = mksem("c_po")
    c_g1 = mksem("c_g1")

    r = Rec()
    W, O = r.wait, r.op
    ch = lambda name, n: DmaChan(r, nc, es, name, n)

    def dump_pre():
        if _dmp["marks"][0] is not None:
            sem, v = _dmp["marks"][0]
            W("scalar", sem, v)
        return g1dump

    def dump_post(sem, val):
        _dmp["marks"][0] = (sem, val)

    all_chans = []

    def ch2(name, n):
        c = ch(name, n)
        all_chans.append(c)
        return c

    ch_x = ch2("dx", 4)
    ch_wg = ch2("dwg", 4)
    ch_pdw = ch2("dpw", 2)
    ch_xtw = ch2("dxw", 2)
    ch_pdr = ch2("dpr", 2)
    ch_xtc = ch2("dxc", 1)
    ch_stw = ch2("dsw", 2)
    ch_str = ch2("dsr", 2)
    ch_xtb = ch2("dxb", 2)
    ch_out = ch2("dou", 2)
    ch_q1 = ch2("dq1", 2)
    ch_q2 = ch2("dq2", 2)
    ch_q3 = ch2("dq3", 2)
    ch_w3 = ch2("dw3", 2)
    ch_s1 = ch2("ds1", 2)
    ch_s2 = ch2("ds2", 2)
    ch_s3 = ch2("ds3", 2)

    O("vector", lambda e: e.memset(onescolb[:], 1.0))
    O("vector", lambda e: e.memset(ones128f[:], 1.0))
    O("vector", lambda e: e.memset(sone[:], 1.0))
    O("vector", lambda e: e.memset(shalf[:], 0.5))
    m_init_ve = O("vector", lambda e: e.memset(squart[:], 0.25), inc=c_ve)
    mpo = O("gpsimd", lambda e: e.memset(identb[:], 0.0), inc=c_po)
    W("gpsimd", c_po, mpo)
    m_init_po = O("gpsimd", lambda e: e.affine_select(
        out=identb[:], in_=identb[:], compare_op=ALU.not_equal, fill=1.0,
        base=0, pattern=[[-1, 128]], channel_multiplier=1), inc=c_po)

    g1_chunks = ([("w1", h, 0) for h in range(NH)]
                 + [("w2", h, 0) for h in range(NH)]
                 + [("w3", i, pp) for i in range(NS) for pp in range(3)])
    G1N = len(g1_chunks)
    g1_state = {"dma": 0, "abs": 0}
    g1_marks = {}

    def g1_emit_dma():
        i = g1_state["dma"]
        if i >= G1N:
            return
        kind, idx, piece = g1_chunks[i]
        slot = i % 4
        if i >= 4:
            W("sync", c_g1, g1_marks[i - 4])
        if kind == "w1":
            ch_wg.dma("sync", slot, lambda e, h=idx, sl=slot: e.dma_start(
                wg[:, sl, :], w1t[h].rearrange("p t i -> p (t i)")))
        elif kind == "w2":
            ch_wg.dma("sync", slot, lambda e, h=idx, sl=slot: e.dma_start(
                wg[:, sl, :], w2t[h].rearrange("p t i -> p (t i)")))
        else:
            lo = piece * 2048
            hi = min(NH * 128, lo + 2048)
            ch_wg.dma("sync", slot, lambda e, I=idx, a=lo, b=hi, sl=slot:
                      e.dma_start(wg[:, sl, 0:b - a],
                                  w3t[I].rearrange("p t i -> p (t i)")[:, a:b]))
        g1_state["dma"] = i + 1

    def g1_emit_abs():
        i = g1_state["abs"]
        if i >= G1N or i >= g1_state["dma"]:
            return
        kind, idx, piece = g1_chunks[i]
        slot = i % 4
        if kind == "w1":
            col, width = idx, 2048
        elif kind == "w2":
            col, width = NH + idx, 2048
        else:
            col = 2 * NH + idx * 3 + piece
            width = 2048 if piece < 2 else 1536
        ch_wg.wait("scalar", slot)
        db = dump_pre()
        g1_marks[i] = O("scalar", lambda e, sl=slot, ww=width, cc=col,
                        db_=db: e.activation(
                            db_[:, 0:ww], wg[:, sl, 0:ww], AF.Abs,
                            accum_out=gcols[:, cc:cc + 1]), inc=c_g1)
        dump_post(c_g1, g1_marks[i])
        g1_state["abs"] = i + 1

    def emit_all(ph_stacks=()):
        for st in ph_stacks:
            st.close()
        for c in all_chans:
            c.wait_all("scalar")
            c.wait_all("sync")
        with es, nc.Block() as blk:
            blk.sync(lambda e: [f(e) for f in r.ops["sync"]])
            blk.scalar(lambda e: [f(e) for f in r.ops["scalar"]])
            blk.vector(lambda e: [f(e) for f in r.ops["vector"]])
            blk.tensor(lambda e: [f(e) for f in r.ops["tensor"]])
            blk.gpsimd(lambda e: [f(e) for f in r.ops["gpsimd"]])
        return nc

    ph01 = ExitStack()
    X = ph01.enter_context(nc.sbuf_tensor("X", [128, NT, 2048], BF16))
    ph0 = ExitStack()
    xstage = ph0.enter_context(nc.sbuf_tensor("xstage", [128, 4, 2048], F32))
    xsqscr = ph0.enter_context(nc.sbuf_tensor("xsqscr", [128, 2048], BF16))
    dps_s = ExitStack()
    dps = dps_s.enter_context(nc.psum_tensor("dps", [128, NS], F32))

    m_apply = {}
    m_sq = {}
    m_dmm = {}
    m_xsq = {}
    m_red = {}

    for g in range(NT // 4):
        cts = list(range(4 * g, 4 * g + 4))
        for ct in cts:
            slot = ct % 4
            q = "scalar" if ct % 2 == 0 else "sync"
            if ct >= 4:
                W(q, c_ve, m_apply[ct - 4])
            ch_x.dma(q, slot, lambda e, c=ct, sl=slot: e.dma_start(
                xstage[:, sl, :], xb[c * 128:(c + 1) * 128, :]))
        g1_emit_dma(); g1_emit_abs()
        g1_emit_dma(); g1_emit_abs()
        for ct in cts:
            slot = ct % 4
            ch_x.wait("vector", slot)
            m_red[ct] = O("vector", lambda e, c=ct, sl=slot: e.reduce_sum(
                sums[:, c:c + 1], xstage[:, sl, :], axis=AX.X), inc=c_ve)
            ch_x.wait("scalar", slot)
            db = dump_pre()
            m_sq[ct] = O("scalar", lambda e, c=ct, sl=slot, db_=db:
                         e.activation(db_[:], xstage[:, sl, :], AF.Square,
                                      accum_out=sqs[:, c:c + 1]), inc=c_act)
            dump_post(c_act, m_sq[ct])
        c0, c1 = 4 * g, 4 * g + 4
        W("vector", c_ve, m_red[c1 - 1])
        W("vector", c_act, m_sq[c1 - 1])
        mv = O("vector", lambda e, a=c0, b=c1: e.tensor_scalar(
            meanc[:, a:b], sums[:, a:b], 1.0 / D, None, ALU.mult), inc=c_ve)
        mv = O("vector", lambda e, a=c0, b=c1: e.tensor_scalar(
            msqc[:, a:b], sqs[:, a:b], 1.0 / D, None, ALU.mult), inc=c_ve)
        W("vector", c_ve, mv)
        mv = O("vector", lambda e, a=c0, b=c1: e.tensor_mul(
            m2c[:, a:b], meanc[:, a:b], meanc[:, a:b]), inc=c_ve)
        W("vector", c_ve, mv)
        mv = O("vector", lambda e, a=c0, b=c1: e.scalar_tensor_tensor(
            vepsc[:, a:b], msqc[:, a:b], 1e-5, m2c[:, a:b],
            ALU.add, ALU.subtract), inc=c_ve)
        W("scalar", c_ve, mv)
        ma = O("scalar", lambda e, a=c0, b=c1: e.activation(
            stdc[:, a:b], vepsc[:, a:b], AF.Sqrt), inc=c_act)
        W("vector", c_act, ma)
        mv = O("vector", lambda e, a=c0, b=c1: e.reciprocal(
            rstdc[:, a:b], stdc[:, a:b]), inc=c_ve)
        W("vector", c_ve, mv)
        W("vector", c_ve, mv)
        for ct in cts:
            slot = ct % 4
            m_apply[ct] = O("vector", lambda e, c=ct, sl=slot: e.tensor_scalar(
                X[:, c, :], xstage[:, sl, :], meanc[:, c:c + 1],
                rstdc[:, c:c + 1], ALU.subtract, ALU.mult), inc=c_ve)
        for ct in cts:
            xeng = "scalar" if ct % 2 == 0 else "vector"
            xsem = c_act if ct % 2 == 0 else c_ve
            W(xeng, c_ve, m_apply[ct])
            if ct >= 1:
                W(xeng, c_pe, m_dmm[ct - 1])
            if ct % 2 == 0:
                m_xsq[ct] = O("scalar", lambda e, c=ct: e.activation(
                    xsqscr[:], X[:, c, :], AF.Square), inc=c_act)
            else:
                m_xsq[ct] = O("vector", lambda e, c=ct: e.tensor_mul(
                    xsqscr[:], X[:, c, :], X[:, c, :]), inc=c_ve)
            W("tensor", xsem, m_xsq[ct])
            if ct == 0:
                W("tensor", c_ve, m_init_ve)
            for s in range(NS):
                last = (s == NS - 1)
                mp = O("tensor", lambda e, ss=s, c=ct: e.matmul(
                    dps[:, ss:ss + 1], xsqscr[:, ss * 128:(ss + 1) * 128],
                    onescolb[:], start=(c == 0 and ss == 0),
                    stop=(c == NT - 1 and ss == NS - 1)),
                    inc=c_pe if last else None)
                if last:
                    m_dmm[ct] = mp
        g1_emit_dma(); g1_emit_abs()

    W("vector", c_pe, m_dmm[NT - 1])
    mv = O("vector", lambda e: e.tensor_copy(dcol[:], dps[:]), inc=c_ve)
    W("vector", c_ve, mv)
    m_ndcol = O("vector", lambda e: e.tensor_scalar(
        ndcol[:], dcol[:], -1.0, None, ALU.mult), inc=c_ve)
    dps_s.close()
    ph0.close()

    if MK_PH < 1:
        return emit_all((ph01,))

    ph1 = ExitStack()
    Pbuf = ph1.enter_context(nc.sbuf_tensor("Pbuf", [128, 2, 2048], BF16))
    atst = ph1.enter_context(nc.sbuf_tensor("atst", [128, 2, 2048], BF16))
    xtst = ph1.enter_context(nc.sbuf_tensor("xtst", [128, 2, 2048], BF16))
    Eps = ph1.enter_context(nc.psum_tensor("Eps", [128, 2048], F32))
    ptp = ph1.enter_context(nc.psum_tensor("ptp", [128, 16, 128], BF16))
    xtp = ph1.enter_context(nc.psum_tensor("xtp", [128, 16, 128], BF16))

    m_exp = {}; m_ptp = {}; m_xttp = {}; m_atev = {}; m_xtev = {}; m_zred = {}
    m_e = {}

    m_exp4 = {}
    for s in range(NS):
        if s == 0:
            W("tensor", c_ve, m_ndcol)
        for n in range(4):
            if s > 0:
                W("tensor", c_act, m_exp4[(s - 1, n)])
            for ct in range(NT):
                last = (ct == NT - 1 and n == 3)
                me = O("tensor", lambda e, c=ct, nn=n, ss=s: e.matmul(
                    Eps[:, nn * 512:(nn + 1) * 512],
                    X[:, c, ss * 128:(ss + 1) * 128],
                    X[:, c, nn * 512:(nn + 1) * 512],
                    start=(c == 0), stop=(c == NT - 1)),
                    inc=c_pe if (ct == NT - 1) else None)
                if ct == NT - 1:
                    m_e4n = me
            m_e[(s, n)] = m_e4n
        if s == 0:
            W("tensor", c_po, m_init_po)
        if s >= 1:
            W("tensor", c_ve, m_xtev[s - 1])
        for ct in range(NTO):
            mx = O("tensor", lambda e, c=ct, ss=s: e.transpose(
                xtp[:, c, :], X[:, c, ss * 128:(ss + 1) * 128], identb[:]),
                inc=c_pe if ct == NTO - 1 else None)
            if ct == NTO - 1:
                m_xttp[s] = mx
        if s == 0:
            W("scalar", c_ve, m_ndcol)
        if s >= 2:
            W("scalar", c_pe, m_ptp[s - 2])
            W("scalar", c_ve, m_zred[s - 2])
        for n in range(4):
            W("scalar", c_pe, m_e[(s, n)])
            m_exp4[(s, n)] = O("scalar", lambda e, ss=s, nn=n: e.activation(
                Pbuf[:, ss % 2, nn * 512:(nn + 1) * 512],
                Eps[:, nn * 512:(nn + 1) * 512], AF.Exp,
                bias=ndcol[:, ss:ss + 1], scale=sone[:]), inc=c_act)
        m_exp[s] = m_exp4[(s, 3)]
        W("vector", c_act, m_exp[s])
        m_zred[s] = O("vector", lambda e, ss=s: e.reduce_sum(
            zcol[:, ss:ss + 1], Pbuf[:, ss % 2, :], axis=AX.X), inc=c_ve)
        W("tensor", c_act, m_exp[s])
        if s >= 1:
            W("tensor", c_act, m_atev[s - 1])
        for t in range(16):
            mp = O("tensor", lambda e, tt=t, ss=s: e.transpose(
                ptp[:, tt, :], Pbuf[:, ss % 2, tt * 128:(tt + 1) * 128],
                identb[:]), inc=c_pe if t == 15 else None)
            if t == 15:
                m_ptp[s] = mp
        W("scalar", c_pe, m_ptp[s])
        if s >= 2:
            ch_pdw.wait("scalar", s % 2)
        m_atev[s] = O("scalar", lambda e, ss=s: e.copy(
            atst[:, ss % 2, :], ptp.rearrange("p a b -> p (a b)")), inc=c_act)
        W("vector", c_pe, m_xttp[s])
        if s >= 2:
            ch_xtw.wait("vector", s % 2)
        m_xtev[s] = O("vector", lambda e, ss=s: e.tensor_copy(
            xtst[:, ss % 2, :], xtp.rearrange("p a b -> p (a b)")), inc=c_ve)
        W("scalar", c_act, m_atev[s])
        ch_pdw.dma("scalar", s % 2, lambda e, ss=s: e.dma_start(
            pd[ss].rearrange("t jj ii -> jj t ii"),
            atst[:, ss % 2, :].rearrange("p (t i) -> p t i", t=16)))
        W("scalar", c_ve, m_xtev[s])
        ch_xtw.dma("scalar", s % 2, lambda e, ss=s: e.dma_start(
            xtd[ss], xtst[:, ss % 2, :]))
        for _ in range(6):
            g1_emit_dma()
            g1_emit_abs()

    W("vector", c_ve, m_zred[NS - 1])
    m_rz = O("vector", lambda e: e.reciprocal(rzcol[:], zcol[:]), inc=c_ve)
    ph1.close()
    ph01.close()

    if MK_PH < 2:
        return emit_all(())

    ph2 = ExitStack()
    hT = ph2.enter_context(nc.sbuf_tensor("hT", [128, NH, CH], BF16))

    m_os = {}; m_sTev = {}; m_sg = {}; m_hmul = {}; m_x1 = {}; m_x2 = {}
    m_q1 = {}; m_q2 = {}; m_sgnq = {}; m_o3 = {}; m_g = {}; m_oev = {}
    m_q3add = {}; m_sgn3 = {}
    m_gamma_done = None

    for k in range(NK if MK_PH >= 5 else 1):
        ph3 = ExitStack()
        XTc = ph3.enter_context(nc.sbuf_tensor(f"XTc{k}", [128, NS, CH], BF16))
        ph3i = ExitStack()
        PIbuf = ph3i.enter_context(
            nc.sbuf_tensor(f"PIbuf{k}", [128, 2, 16, 128], BF16))
        sTst = ph3i.enter_context(nc.sbuf_tensor(f"sTst{k}", [128, 2, CH], F32))
        osps = ph3i.enter_context(nc.psum_tensor(f"osps{k}", [128, 2, CH], F32))
        gps = ph3i.enter_context(nc.psum_tensor(f"gps{k}", [128, 4], F32))

        for cch in (ch_pdw, ch_xtw, ch_out, ch_s3, ch_q3, ch_w3, ch_str,
                    ch_xtb, ch_stw):
            cch.wait_all("scalar")
        W("scalar", c_pe, c_pe.n)
        W("scalar", c_act, c_act.n)
        W("scalar", c_ve, c_ve.n)
        W("sync", c_pe, c_pe.n)
        W("sync", c_ve, c_ve.n)
        W("sync", c_act, c_act.n)
        for cch in (ch_pdw, ch_xtw, ch_out, ch_s3, ch_q3, ch_w3, ch_str,
                    ch_xtb, ch_stw, ch_s1, ch_s2, ch_q1, ch_q2):
            cch.wait_all("sync")
        for s in range(NS):
            ch_xtc.dma("scalar" if s % 2 == 0 else "sync", 0,
                       lambda e, ss=s, kk=k, T=XTc: e.dma_start(
                T[:, ss, :], xtd[ss][:, kk * CH:(kk + 1) * CH]))
        xtc_sem, xtc_val = ch_xtc.sems[0], ch_xtc.sems[0].n

        ch_pdr.dma("scalar", 0, lambda e, T=PIbuf: e.dma_start(
            T[:, 0, :, :], pd[0].rearrange("t jj ii -> jj t ii")))
        for I in range(NS):
            if I + 1 < NS:
                if I >= 1:
                    W("scalar", c_pe, m_os[(k, I - 1)])
                ch_pdr.dma("scalar", (I + 1) % 2, lambda e, II=I + 1, T=PIbuf:
                           e.dma_start(
                    T[:, II % 2, :, :],
                    pd[II].rearrange("t jj ii -> jj t ii")))
            ch_pdr.wait("tensor", I % 2)
            W("tensor", xtc_sem, xtc_val)
            if I >= 2:
                W("tensor", c_ve, m_sTev[(k, I - 2)])
            elif k > 0:
                W("tensor", c_act, c_act.n)
            if k == 0 and I == 0:
                W("tensor", c_ve, m_rz)
            for t in range(NS):
                for n in range(2):
                    last = (t == NS - 1 and n == 1)
                    mo = O("tensor", lambda e, tt=t, nn=n, II=I, Ta=osps,
                           Tb=PIbuf, Tc=XTc: e.matmul(
                        Ta[:, II % 2, nn * 512:(nn + 1) * 512],
                        Tb[:, II % 2, tt, :],
                        Tc[:, tt, nn * 512:(nn + 1) * 512],
                        start=(tt == 0), stop=(tt == NS - 1)),
                        inc=c_pe if last else None)
                    if last:
                        m_os[(k, I)] = mo
            W("vector", c_pe, m_os[(k, I)])
            if I >= 2:
                ch_stw.wait("vector", I % 2)
            m_sTev[(k, I)] = O("vector", lambda e, II=I, Ta=sTst, Tb=osps:
                e.tensor_scalar(
                Ta[:, II % 2, :], Tb[:, II % 2, :],
                rzcol[:, II:II + 1], None, ALU.mult), inc=c_ve)
            W("scalar", c_ve, m_sTev[(k, I)])
            ch_stw.dma("scalar", I % 2, lambda e, II=I, kk=k, T=sTst:
                e.dma_start(
                sTd[II][:, kk * CH:(kk + 1) * CH], T[:, II % 2, :]))
            if k == 0:
                for _ in range(3):
                    g1_emit_dma()
                    g1_emit_abs()

        if k == 0:
            while g1_state["abs"] < G1N:
                g1_emit_dma()
                g1_emit_abs()
            W("vector", c_g1, c_g1.n)
            O("vector", lambda e: e.reduce_sum(
                gsum3[:, 0:1], gcols[:, 0:NH], axis=AX.X), inc=c_ve)
            O("vector", lambda e: e.reduce_sum(
                gsum3[:, 1:2], gcols[:, NH:2 * NH], axis=AX.X), inc=c_ve)
            mv = O("vector", lambda e: e.reduce_sum(
                gsum3[:, 2:3], gcols[:, 2 * NH:2 * NH + 3 * NS], axis=AX.X),
                inc=c_ve)
            W("tensor", c_ve, mv)
            mp = O("tensor", lambda e, T=gps: e.matmul(
                T[:, 0:3], ones128f[:], gsum3[:, 0:3], start=True, stop=True),
                inc=c_pe)
            W("vector", c_pe, mp)
            mv = O("vector", lambda e, T=gps: e.tensor_copy(
                grep3[:, 0:3], T[:, 0:3]), inc=c_ve)
            W("vector", c_ve, mv)
            mv = O("vector", lambda e: e.tensor_scalar(
                t1g[:, 0:3], grep3[:, 0:3], 1.0 / NW, 1e-8, ALU.mult, ALU.add),
                inc=c_ve)
            W("vector", c_ve, mv)
            O("vector", lambda e: e.tensor_scalar(
                tpos3[:, 0:3], t1g[:, 0:3], 0.5, None, ALU.mult), inc=c_ve)
            m_gamma_done = O("vector", lambda e: e.tensor_scalar(
                tneg3[:, 0:3], t1g[:, 0:3], -0.5, None, ALU.mult), inc=c_ve)

        ph3i.close()
        if MK_PH < 3:
            ph3.close()
            break

        ph4a = ExitStack()
        Q1s = ph4a.enter_context(nc.sbuf_tensor(f"Q1s{k}", [128, 2, 16, 128], BF16))
        Q2s = ph4a.enter_context(nc.sbuf_tensor(f"Q2s{k}", [128, 2, 16, 128], BF16))
        s1st = ph4a.enter_context(nc.sbuf_tensor(f"s1st{k}", [128, 2, 2048], BF16))
        s2st = ph4a.enter_context(nc.sbuf_tensor(f"s2st{k}", [128, 2, 2048], BF16))
        sgst = ph4a.enter_context(nc.sbuf_tensor(f"sgst{k}", [128, 2, CH], F32))
        x1ps = ph4a.enter_context(nc.psum_tensor(f"x1ps{k}", [128, 2, CH], F32))
        x2ps = ph4a.enter_context(nc.psum_tensor(f"x2ps{k}", [128, 2, CH], F32))

        pe_after_ph3 = c_pe.n
        ve_after_ph3 = c_ve.n
        act_after_ph3 = c_act.n
        W("sync", c_pe, c_pe.n)
        W("sync", c_ve, c_ve.n)
        W("sync", c_act, c_act.n)
        ch_out.wait_all("sync")
        ch_str.wait_all("sync")
        ch_xtb.wait_all("sync")
        ch_stw.wait_all("sync")
        ch_stw.wait_all("vector")
        ch_pdr.wait_all("vector")
        W("vector", c_pe, pe_after_ph3)
        ch_stw.wait_all("scalar")
        ch_pdr.wait_all("scalar")

        for h in range(NH):
            slot = h % 2
            if k == 0:
                wsl1, wsl2 = (2 * h) % 4, (2 * h + 1) % 4
                if h >= 2:
                    W("sync", c_act, m_sgnq[(h - 2, 1)])
                    W("sync", c_ve, m_sgnq[(h - 2, 2)])
                ch_wg.dma("sync", wsl1, lambda e, hh=h, sl=wsl1: e.dma_start(
                    wg[:, sl, :], w1t[hh].rearrange("p t i -> p (t i)")))
                ch_wg.dma("sync", wsl2, lambda e, hh=h, sl=wsl2: e.dma_start(
                    wg[:, sl, :], w2t[hh].rearrange("p t i -> p (t i)")))
                ch_wg.wait("scalar", wsl1)
                if h == 0:
                    W("scalar", c_ve, m_gamma_done)
                    W("scalar", c_pe, pe_after_ph3)
                if h >= 2:
                    W("scalar", c_ve, m_q1[(k, h - 2)])
                    W("scalar", c_ve, m_q2[(k, h - 2)])
                O("scalar", lambda e, sl=wsl1, so=slot,
                    T=s1st: e.sign(
                    T[:, so, :], wg[:, sl, :], bias=tneg3[:, 0:1]),
                    inc=c_act)
                ma = O("scalar", lambda e, sl=wsl1, so=slot, T=s2st: e.sign(
                    T[:, so, :], wg[:, sl, :], bias=tpos3[:, 0:1]),
                    inc=c_act)
                m_sgnq[(h, 1)] = ma
                W("vector", c_act, ma)
                if h >= 2:
                    W("vector", c_pe, m_x1[(k, h - 2)])
                    ch_s1.wait("vector", slot)
                m_q1[(k, h)] = O("vector", lambda e, so=slot, Ta=Q1s, Tb=s1st,
                    Tc=s2st: e.tensor_add(
                    Ta[:, so, :, :].rearrange("p a b -> p (a b)"),
                    Tb[:, so, :], Tc[:, so, :]), inc=c_ve)
                ch_wg.wait("vector", wsl2)
                W("vector", c_ve, m_q1[(k, h)])
                O("vector", lambda e, sl=wsl2, so=slot, T=s1st: e.tensor_scalar(
                    T[:, so, :], wg[:, sl, :], tpos3[:, 1:2], None, ALU.is_gt),
                    inc=c_ve)
                mq2m = O("vector", lambda e, sl=wsl2, so=slot, T=s2st:
                    e.tensor_scalar(
                    T[:, so, :], wg[:, sl, :], tneg3[:, 1:2], None, ALU.is_lt),
                    inc=c_ve)
                m_sgnq[(h, 2)] = mq2m
                W("vector", c_ve, mq2m)
                if h >= 2:
                    W("vector", c_pe, m_x2[(k, h - 2)])
                    ch_s2.wait("vector", slot)
                m_q2[(k, h)] = O("vector", lambda e, so=slot, Ta=Q2s, Tb=s1st,
                    Tc=s2st: e.tensor_sub(
                    Ta[:, so, :, :].rearrange("p a b -> p (a b)"),
                    Tb[:, so, :], Tc[:, so, :]), inc=c_ve)
                W("scalar", c_ve, m_q1[(k, h)])
                ch_s1.dma("scalar", slot, lambda e, hh=h, so=slot, T=Q1s:
                    e.dma_start(
                    qd1[hh].rearrange("p t i -> p (t i)"),
                    T[:, so, :, :].rearrange("p a b -> p (a b)")))
                W("scalar", c_ve, m_q2[(k, h)])
                ch_s2.dma("scalar", slot, lambda e, hh=h, so=slot, T=Q2s:
                    e.dma_start(
                    qd2[hh].rearrange("p t i -> p (t i)"),
                    T[:, so, :, :].rearrange("p a b -> p (a b)")))
            else:
                for hh2 in ([0, 1] if h == 0 else
                            [h + 1] if h + 1 < NH else []):
                    so2 = hh2 % 2
                    if hh2 >= 2:
                        W("sync", c_pe, m_x2[(k, hh2 - 2)])
                    ch_q1.dma("sync", so2, lambda e, hh=hh2, so=so2, T=Q1s:
                        e.dma_start(
                        T[:, so, :, :].rearrange("p a b -> p (a b)"),
                        qd1[hh].rearrange("p t i -> p (t i)")))
                    ch_q2.dma("sync", so2, lambda e, hh=hh2, so=so2, T=Q2s:
                        e.dma_start(
                        T[:, so, :, :].rearrange("p a b -> p (a b)"),
                        qd2[hh].rearrange("p t i -> p (t i)")))
            if k == 0:
                W("tensor", c_ve, m_q1[(k, h)])
            else:
                ch_q1.wait("tensor", slot)
            if h == 0:
                W("tensor", c_pe, pe_after_ph3)
                W("tensor", c_ve, ve_after_ph3)
                W("tensor", c_act, act_after_ph3)
                W("tensor", xtc_sem, xtc_val)
            if h >= 2:
                W("tensor", c_act, m_sg[(k, h - 2)])
            for t in range(NS):
                for n in range(2):
                    last = (t == NS - 1 and n == 1)
                    mm_ = O("tensor", lambda e, tt=t, nn=n, so=slot, Ta=x1ps,
                            Tb=Q1s, Tc=XTc: e.matmul(
                        Ta[:, so, nn * 512:(nn + 1) * 512],
                        Tb[:, so, tt, :],
                        Tc[:, tt, nn * 512:(nn + 1) * 512],
                        start=(tt == 0), stop=(tt == NS - 1)),
                        inc=c_pe if last else None)
                    if last:
                        m_x1[(k, h)] = mm_
            if k == 0:
                W("tensor", c_ve, m_q2[(k, h)])
            else:
                ch_q2.wait("tensor", slot)
            if h >= 2:
                W("tensor", c_ve, m_hmul[(k, h - 2)])
            for t in range(NS):
                for n in range(2):
                    last = (t == NS - 1 and n == 1)
                    mm_ = O("tensor", lambda e, tt=t, nn=n, so=slot, Ta=x2ps,
                            Tb=Q2s, Tc=XTc: e.matmul(
                        Ta[:, so, nn * 512:(nn + 1) * 512],
                        Tb[:, so, tt, :],
                        Tc[:, tt, nn * 512:(nn + 1) * 512],
                        start=(tt == 0), stop=(tt == NS - 1)),
                        inc=c_pe if last else None)
                    if last:
                        m_x2[(k, h)] = mm_
            for hh in ([h - 1] if h >= 1 else []) + ([h] if h == NH - 1 else []):
                so2 = hh % 2
                W("scalar", c_pe, m_x1[(k, hh)])
                if hh >= 2:
                    W("scalar", c_ve, m_hmul[(k, hh - 2)])
                m_sg[(k, hh)] = O("scalar", lambda e, so=so2, Ta=sgst, Tb=x1ps:
                    e.activation(
                    Ta[:, so, :], Tb[:, so, :], AF.Sigmoid, scale=shalf[:]),
                    inc=c_act)
                W("vector", c_act, m_sg[(k, hh)])
                W("vector", c_pe, m_x2[(k, hh)])
                m_hmul[(k, hh)] = O("vector", lambda e, hh_=hh, so=so2, Ta=sgst,
                    Tb=x2ps: e.tensor_mul(
                    hT[:, hh_, :], Ta[:, so, :], Tb[:, so, :]), inc=c_ve)

        ph4a.close()
        ph3.close()
        if MK_PH < 4:
            break

        ph4b = ExitStack()
        Q3s = ph4b.enter_context(nc.sbuf_tensor(f"Q3s{k}", [128, 2, NH, 128], BF16))
        w3st = ph4b.enter_context(nc.sbuf_tensor(f"w3st{k}", [128, 2, 11, 128], F32))
        s3a = ph4b.enter_context(nc.sbuf_tensor(f"s3a{k}", [128, 2, 1408], BF16))
        s3b = ph4b.enter_context(nc.sbuf_tensor(f"s3b{k}", [128, 2, 1408], BF16))
        gst = ph4b.enter_context(nc.sbuf_tensor(f"gst{k}", [128, 2, CH], F32))
        sTs = ph4b.enter_context(nc.sbuf_tensor(f"sTs{k}", [128, 2, CH], F32))
        xts = ph4b.enter_context(nc.sbuf_tensor(f"xts{k}", [128, 2, CH], F32))
        cb1 = ph4b.enter_context(nc.sbuf_tensor(f"cb1{k}", [128, 2, CH], F32))
        o3ps = ph4b.enter_context(nc.psum_tensor(f"o3ps{k}", [128, 2, CH], F32))

        pe_after_4a = c_pe.n
        act_after_4a = c_act.n
        ve_after_4a = c_ve.n
        for eng in ("sync", "scalar", "vector"):
            W(eng, c_pe, c_pe.n)
            W(eng, c_ve, c_ve.n)
            W(eng, c_act, c_act.n)
            ch_s1.wait_all(eng)
            ch_s2.wait_all(eng)
            ch_q1.wait_all(eng)
            ch_q2.wait_all(eng)
        for I in range(NS):
            slot = I % 2
            if k == 0:
                for q in range(4):
                    j = I * 4 + q
                    qs = j % 2
                    lo = q * 11
                    hi = min(NH, lo + 11)
                    wd = (hi - lo) * 128
                    if j >= 2:
                        pi, pq = divmod(j - 2, 4)
                        W("sync", c_ve, m_sgn3[(pi, pq)])
                    ch_w3.dma("sync", qs, lambda e, II=I, a=lo, b=hi, ss=qs,
                              T=w3st: e.dma_start(
                        T[:, ss, 0:b - a, :].rearrange("p a b -> p (a b)"),
                        w3t[II][:, a:b, :].rearrange("p a b -> p (a b)")))
                    ch_w3.wait("vector", qs)
                    if I == 0 and q == 0:
                        W("vector", c_ve, m_gamma_done)
                        W("vector", c_pe, pe_after_4a)
                    if j >= 2:
                        pi, pq = divmod(j - 2, 4)
                        W("vector", c_ve, m_q3add[(pi, pq)])
                    O("vector", lambda e, ss=qs, ww=wd, Ta=s3a, Tb=w3st:
                        e.tensor_scalar(
                        Ta[:, ss, 0:ww],
                        Tb[:, ss, :, :].rearrange("p a b -> p (a b)")[:, 0:ww],
                        tpos3[:, 2:3], None, ALU.is_gt), inc=c_ve)
                    m_sgn3[(I, q)] = O("vector", lambda e, ss=qs, ww=wd,
                        Ta=s3b, Tb=w3st: e.tensor_scalar(
                        Ta[:, ss, 0:ww],
                        Tb[:, ss, :, :].rearrange("p a b -> p (a b)")[:, 0:ww],
                        tneg3[:, 2:3], None, ALU.is_lt), inc=c_ve)
                    W("vector", c_ve, m_sgn3[(I, q)])
                    if I >= 2 and q == 0:
                        W("vector", c_pe, m_o3[(k, I - 2)])
                        ch_s3.wait("vector", slot)
                    m_q3add[(I, q)] = O("vector",
                        lambda e, a=lo, b=hi, ss=qs, so=slot, ww=wd, Ta=Q3s,
                        Tb=s3a, Tc=s3b:
                        e.tensor_sub(
                            Ta[:, so, a:b, :].rearrange("p a b -> p (a b)"),
                            Tb[:, ss, 0:ww], Tc[:, ss, 0:ww]), inc=c_ve)
                W("scalar", c_ve, m_q3add[(I, 3)])
                ch_s3.dma("scalar", slot, lambda e, II=I, so=slot, T=Q3s:
                    e.dma_start(
                    qd3[II].rearrange("p t i -> p (t i)"),
                    T[:, so, :, :].rearrange("p a b -> p (a b)")))
            else:
                for II2 in ([0, 1] if I == 0 else
                            [I + 1] if I + 1 < NS else []):
                    so2 = II2 % 2
                    if II2 >= 2:
                        W("sync", c_pe, m_o3[(k, II2 - 2)])
                    ch_q3.dma("sync", so2, lambda e, II=II2, so=so2, T=Q3s:
                        e.dma_start(
                        T[:, so, :, :].rearrange("p a b -> p (a b)"),
                        qd3[II].rearrange("p t i -> p (t i)")))
            if I == 0:
                W("scalar", c_ve, ve_after_4a)
                W("scalar", c_act, act_after_4a)
                W("scalar", c_pe, pe_after_4a)
            if I >= 2:
                W("scalar", c_ve, m_oev[(k, I - 2)])
                ch_out.wait("scalar", slot)
            ch_stw.wait("scalar", slot)
            ch_str.dma("scalar", slot, lambda e, II=I, kk=k, T=sTs:
                e.dma_start(
                T[:, II % 2, :], sTd[II][:, kk * CH:(kk + 1) * CH]))
            ch_xtb.dma("scalar", slot, lambda e, II=I, kk=k, T=xts:
                e.dma_start(
                T[:, II % 2, :],
                xtb[II * 128:(II + 1) * 128, kk * CH:(kk + 1) * CH]))
            if k == 0:
                W("tensor", c_ve, m_q3add[(I, 3)])
            else:
                ch_q3.wait("tensor", slot)
            if I == 0:
                W("tensor", c_pe, pe_after_4a)
                W("tensor", c_act, act_after_4a)
                W("tensor", c_ve, ve_after_4a)
            if I >= 2:
                W("tensor", c_act, m_g[(k, I - 2)])
            for t in range(NH):
                for n in range(2):
                    last = (t == NH - 1 and n == 1)
                    mm_ = O("tensor", lambda e, tt=t, nn=n, so=slot, Ta=o3ps,
                            Tb=Q3s: e.matmul(
                        Ta[:, so, nn * 512:(nn + 1) * 512],
                        Tb[:, so, tt, :],
                        hT[:, tt, nn * 512:(nn + 1) * 512],
                        start=(tt == 0), stop=(tt == NH - 1)),
                        inc=c_pe if last else None)
                    if last:
                        m_o3[(k, I)] = mm_
            for hh in ([I - 1] if I >= 1 else []) + ([I] if I == NS - 1 else []):
                so2 = hh % 2
                W("scalar", c_pe, m_o3[(k, hh)])
                if hh >= 2:
                    W("scalar", c_ve, m_oev[(k, hh - 2)])
                m_g[(k, hh)] = O("scalar", lambda e, so=so2, Ta=gst, Tb=o3ps:
                    e.activation(
                    Ta[:, so, :], Tb[:, so, :], AF.Silu, scale=sone[:]),
                    inc=c_act)
                W("vector", c_act, m_g[(k, hh)])
                ch_str.wait("vector", so2)
                ch_xtb.wait("vector", so2)
                if hh >= 2:
                    ch_out.wait("vector", so2)
                mv = O("vector", lambda e, so=so2, Ta=cb1, Tb=gst, Tc=sTs:
                    e.tensor_mul(
                    Ta[:, so, :], Tb[:, so, :], Tc[:, so, :]), inc=c_ve)
                W("vector", c_ve, mv)
                m_oev[(k, hh)] = O("vector", lambda e, so=so2, Ta=sTs, Tb=cb1,
                    Tc=xts: e.tensor_add(
                    Ta[:, so, :], Tb[:, so, :], Tc[:, so, :]), inc=c_ve)
                W("scalar", c_ve, m_oev[(k, hh)])
                ch_out.dma("scalar", so2, lambda e, II=hh, kk=k, T=sTs:
                    e.dma_start(
                    outT[II * 128:(II + 1) * 128, kk * CH:(kk + 1) * CH],
                    T[:, II % 2, :]))

        ph4b.close()

    ch_out.wait_all("scalar")
    for c in (ch_s1, ch_s2, ch_s3, ch_q1, ch_q2, ch_q3, ch_w3, ch_wg):
        c.wait_all("sync")

    ph2.close()
    return emit_all(())


def kernel(L0_attn_x, L0_ffn_w1, L0_ffn_w2, L0_ffn_w3):
    x = np.ascontiguousarray(np.asarray(L0_attn_x, dtype=np.float32))
    w1 = np.asarray(L0_ffn_w1, dtype=np.float32)
    w2 = np.asarray(L0_ffn_w2, dtype=np.float32)
    w3 = np.asarray(L0_ffn_w3, dtype=np.float32)

    w1t = np.ascontiguousarray(w1.reshape(NH, 128, 16, 128).transpose(0, 3, 2, 1))
    w2t = np.ascontiguousarray(w2.reshape(NH, 128, 16, 128).transpose(0, 3, 2, 1))
    w3t = np.ascontiguousarray(w3.reshape(NS, 128, NH, 128).transpose(0, 3, 2, 1))

    in_maps = []
    for core in range(8):
        b, p = divmod(core, 2)
        own = x[b, p * S_OWN:(p + 1) * S_OWN]
        oth = x[b, (1 - p) * S_OWN:(2 - p) * S_OWN]
        xbc = np.ascontiguousarray(np.concatenate([own, oth], axis=0))
        xtbc = np.ascontiguousarray(own.T)
        in_maps.append({"xb": xbc, "xtb": xtbc, "w1t": w1t, "w2t": w2t,
                        "w3t": w3t})

    nc = build_program()
    res = run_bass_kernel_spmd(nc, in_maps, core_ids=list(range(8)))

    out = np.empty((4, S_ALL, D), dtype=np.float32)
    for core in range(8):
        b, p = divmod(core, 2)
        out[b, p * S_OWN:(p + 1) * S_OWN, :] = res.results[core]["outT"].T
    return out


# version 13
# speedup vs baseline: 1.0563x; 1.0563x over previous
import os
import numpy as np
from contextlib import ExitStack

import concourse.bass as bass
import concourse.mybir as mybir
from concourse.bass_utils import run_bass_kernel_spmd

MK_PH = int(os.environ.get("MK_PH", "99"))

F32 = mybir.dt.float32
BF16 = mybir.dt.bfloat16
AF = mybir.ActivationFunctionType
ALU = mybir.AluOpType
AX = mybir.AxisListType

D = 2048
S_ALL = 4096
S_OWN = 2048
H = 5632
NT = 32
NTO = 16
NS = 16
NH = 44
CH = 1024
NK = 2
NW = float(H * D)


class Sem:
    def __init__(self, h):
        self.h = h
        self.n = 0


class Rec:
    ENGS = ("sync", "scalar", "vector", "tensor", "gpsimd")

    def __init__(self):
        self.ops = {k: [] for k in self.ENGS}

    def emit(self, eng, f):
        self.ops[eng].append(f)

    def wait(self, eng, sem, v=None):
        v = sem.n if v is None else v
        assert 0 <= v <= sem.n, f"wait beyond emitted count: {v} > {sem.n}"
        if v > 0:
            self.emit(eng, lambda e, s=sem.h, vv=v: e.wait_ge(s, vv))

    def op(self, eng, f, inc=None, k=1):
        if inc is None:
            self.emit(eng, f)
            return None
        inc.n += k
        self.emit(eng, lambda e, ff=f, s=inc.h, kk=k: ff(e).then_inc(s, kk))
        return inc.n


class DmaChan:

    def __init__(self, rec, nc, es, name, nslots):
        self.r = rec
        self.sems = [Sem(es.enter_context(nc.semaphore(f"{name}{i}")))
                     for i in range(nslots)]

    def dma(self, eng, slot, f):
        return self.r.op(eng, f, inc=self.sems[slot], k=16)

    def wait(self, eng, slot, v=None):
        self.r.wait(eng, self.sems[slot], v)

    def wait_all(self, eng):
        for s in self.sems:
            self.r.wait(eng, s)


def build_program():
    nc = bass.Bass("TRN2", target_bir_lowering=False)

    xb = nc.dram_tensor("xb", [S_ALL, D], F32, kind="ExternalInput")
    xtb = nc.dram_tensor("xtb", [D, S_OWN], F32, kind="ExternalInput")
    w1t = nc.dram_tensor("w1t", [NH, 128, 16, 128], F32, kind="ExternalInput")
    w2t = nc.dram_tensor("w2t", [NH, 128, 16, 128], F32, kind="ExternalInput")
    w3t = nc.dram_tensor("w3t", [NS, 128, NH, 128], F32, kind="ExternalInput")
    outT = nc.dram_tensor("outT", [D, S_OWN], F32, kind="ExternalOutput")

    xtd = nc.dram_tensor("xtd", [NS, 128, S_OWN], BF16, kind="Internal")
    pd = nc.dram_tensor("pd", [NS, 16, 128, 128], BF16, kind="Internal")
    sTd = nc.dram_tensor("sTd", [NS, 128, S_OWN], F32, kind="Internal")
    qd1 = nc.dram_tensor("qd1", [NH, 128, 16, 128], BF16, kind="Internal")
    qd2 = nc.dram_tensor("qd2", [NH, 128, 16, 128], BF16, kind="Internal")
    qd3 = nc.dram_tensor("qd3", [NS, 128, NH, 128], BF16, kind="Internal")

    es = ExitStack()
    mksem = lambda n: Sem(es.enter_context(nc.semaphore(n)))

    wg = es.enter_context(nc.sbuf_tensor("wg", [128, 4, 2048], F32))
    sums = es.enter_context(nc.sbuf_tensor("sums", [128, NT], F32))
    sqs = es.enter_context(nc.sbuf_tensor("sqs", [128, NT], F32))
    meanc = es.enter_context(nc.sbuf_tensor("meanc", [128, NT], F32))
    msqc = es.enter_context(nc.sbuf_tensor("msqc", [128, NT], F32))
    m2c = es.enter_context(nc.sbuf_tensor("m2c", [128, NT], F32))
    vepsc = es.enter_context(nc.sbuf_tensor("vepsc", [128, NT], F32))
    stdc = es.enter_context(nc.sbuf_tensor("stdc", [128, NT], F32))
    rstdc = es.enter_context(nc.sbuf_tensor("rstdc", [128, NT], F32))
    zcol = es.enter_context(nc.sbuf_tensor("zcol", [128, NS], F32))
    rzcol = es.enter_context(nc.sbuf_tensor("rzcol", [128, NS], F32))
    dcol = es.enter_context(nc.sbuf_tensor("dcol", [128, NS], F32))
    ndcol = es.enter_context(nc.sbuf_tensor("ndcol", [128, NS], F32))
    gcols = es.enter_context(nc.sbuf_tensor("gcols", [128, 2 * NH + 3 * NS], F32))
    gsum3 = es.enter_context(nc.sbuf_tensor("gsum3", [128, 4], F32))
    grep3 = es.enter_context(nc.sbuf_tensor("grep3", [128, 4], F32))
    t1g = es.enter_context(nc.sbuf_tensor("t1g", [128, 4], F32))
    tpos3 = es.enter_context(nc.sbuf_tensor("tpos3", [128, 4], F32))
    tneg3 = es.enter_context(nc.sbuf_tensor("tneg3", [128, 4], F32))
    onescolb = es.enter_context(nc.sbuf_tensor("onescolb", [128, 1], BF16))
    ones128f = es.enter_context(nc.sbuf_tensor("ones128f", [128, 128], F32))
    identb = es.enter_context(nc.sbuf_tensor("identb", [128, 128], BF16))
    sone = es.enter_context(nc.sbuf_tensor("sone", [128, 1], F32))
    shalf = es.enter_context(nc.sbuf_tensor("shalf", [128, 1], F32))
    squart = es.enter_context(nc.sbuf_tensor("squart", [128, 1], F32))
    g1dump = es.enter_context(nc.sbuf_tensor("g1dump", [128, 2048], F32))

    _dmp = {"i": 0, "marks": [None, None]}

    c_pe = mksem("c_pe")
    c_act = mksem("c_act")
    c_ve = mksem("c_ve")
    c_po = mksem("c_po")
    c_g1 = mksem("c_g1")

    r = Rec()
    W, O = r.wait, r.op
    ch = lambda name, n: DmaChan(r, nc, es, name, n)

    def dump_pre():
        if _dmp["marks"][0] is not None:
            sem, v = _dmp["marks"][0]
            W("scalar", sem, v)
        return g1dump

    def dump_post(sem, val):
        _dmp["marks"][0] = (sem, val)

    all_chans = []

    def ch2(name, n):
        c = ch(name, n)
        all_chans.append(c)
        return c

    ch_x = ch2("dx", 4)
    ch_wg = ch2("dwg", 4)
    ch_pdw = ch2("dpw", 2)
    ch_xtw = ch2("dxw", 2)
    ch_pdr = ch2("dpr", 2)
    ch_xtc = ch2("dxc", 1)
    ch_stw = ch2("dsw", 2)
    ch_str = ch2("dsr", 2)
    ch_xtb = ch2("dxb", 2)
    ch_out = ch2("dou", 2)
    ch_q1 = ch2("dq1", 2)
    ch_q2 = ch2("dq2", 2)
    ch_q3 = ch2("dq3", 2)
    ch_w3 = ch2("dw3", 2)
    ch_s1 = ch2("ds1", 2)
    ch_s2 = ch2("ds2", 2)
    ch_s3 = ch2("ds3", 2)

    O("vector", lambda e: e.memset(onescolb[:], 1.0))
    O("vector", lambda e: e.memset(ones128f[:], 1.0))
    O("vector", lambda e: e.memset(sone[:], 1.0))
    O("vector", lambda e: e.memset(shalf[:], 0.5))
    m_init_ve = O("vector", lambda e: e.memset(squart[:], 0.25), inc=c_ve)
    mpo = O("gpsimd", lambda e: e.memset(identb[:], 0.0), inc=c_po)
    W("gpsimd", c_po, mpo)
    m_init_po = O("gpsimd", lambda e: e.affine_select(
        out=identb[:], in_=identb[:], compare_op=ALU.not_equal, fill=1.0,
        base=0, pattern=[[-1, 128]], channel_multiplier=1), inc=c_po)

    g1_chunks = ([("w1", h, 0) for h in range(NH)]
                 + [("w2", h, 0) for h in range(NH)]
                 + [("w3", i, pp) for i in range(NS) for pp in range(3)])
    G1N = len(g1_chunks)
    g1_state = {"dma": 0, "abs": 0}
    g1_marks = {}

    def g1_emit_dma():
        i = g1_state["dma"]
        if i >= G1N:
            return
        kind, idx, piece = g1_chunks[i]
        slot = i % 4
        if i >= 4:
            W("sync", c_g1, g1_marks[i - 4])
        if kind == "w1":
            ch_wg.dma("sync", slot, lambda e, h=idx, sl=slot: e.dma_start(
                wg[:, sl, :], w1t[h].rearrange("p t i -> p (t i)")))
        elif kind == "w2":
            ch_wg.dma("sync", slot, lambda e, h=idx, sl=slot: e.dma_start(
                wg[:, sl, :], w2t[h].rearrange("p t i -> p (t i)")))
        else:
            lo = piece * 2048
            hi = min(NH * 128, lo + 2048)
            ch_wg.dma("sync", slot, lambda e, I=idx, a=lo, b=hi, sl=slot:
                      e.dma_start(wg[:, sl, 0:b - a],
                                  w3t[I].rearrange("p t i -> p (t i)")[:, a:b]))
        g1_state["dma"] = i + 1

    def g1_emit_abs():
        i = g1_state["abs"]
        if i >= G1N or i >= g1_state["dma"]:
            return
        kind, idx, piece = g1_chunks[i]
        slot = i % 4
        if kind == "w1":
            col, width = idx, 2048
        elif kind == "w2":
            col, width = NH + idx, 2048
        else:
            col = 2 * NH + idx * 3 + piece
            width = 2048 if piece < 2 else 1536
        ch_wg.wait("scalar", slot)
        db = dump_pre()
        g1_marks[i] = O("scalar", lambda e, sl=slot, ww=width, cc=col,
                        db_=db: e.activation(
                            db_[:, 0:ww], wg[:, sl, 0:ww], AF.Abs,
                            accum_out=gcols[:, cc:cc + 1]), inc=c_g1)
        dump_post(c_g1, g1_marks[i])
        g1_state["abs"] = i + 1

    def emit_all(ph_stacks=()):
        for st in ph_stacks:
            st.close()
        for c in all_chans:
            c.wait_all("scalar")
            c.wait_all("sync")
        with es, nc.Block() as blk:
            blk.sync(lambda e: [f(e) for f in r.ops["sync"]])
            blk.scalar(lambda e: [f(e) for f in r.ops["scalar"]])
            blk.vector(lambda e: [f(e) for f in r.ops["vector"]])
            blk.tensor(lambda e: [f(e) for f in r.ops["tensor"]])
            blk.gpsimd(lambda e: [f(e) for f in r.ops["gpsimd"]])
        return nc

    ph01 = ExitStack()
    X = ph01.enter_context(nc.sbuf_tensor("X", [128, NT, 2048], BF16))
    ph0 = ExitStack()
    xstage = ph0.enter_context(nc.sbuf_tensor("xstage", [128, 4, 2048], F32))
    xsqscr = ph0.enter_context(nc.sbuf_tensor("xsqscr", [128, 2048], BF16))
    dps_s = ExitStack()
    dps = dps_s.enter_context(nc.psum_tensor("dps", [128, NS], F32))

    m_apply = {}
    m_sq = {}
    m_dmm = {}
    m_xsq = {}
    m_red = {}

    for g in range(NT // 4):
        cts = list(range(4 * g, 4 * g + 4))
        for ct in cts:
            slot = ct % 4
            q = "scalar" if ct % 2 == 0 else "sync"
            if ct >= 4:
                W(q, c_ve, m_apply[ct - 4])
            ch_x.dma(q, slot, lambda e, c=ct, sl=slot: e.dma_start(
                xstage[:, sl, :], xb[c * 128:(c + 1) * 128, :]))
        g1_emit_dma(); g1_emit_abs()
        g1_emit_dma(); g1_emit_abs()
        for ct in cts:
            slot = ct % 4
            ch_x.wait("vector", slot)
            m_red[ct] = O("vector", lambda e, c=ct, sl=slot: e.reduce_sum(
                sums[:, c:c + 1], xstage[:, sl, :], axis=AX.X), inc=c_ve)
            ch_x.wait("scalar", slot)
            db = dump_pre()
            m_sq[ct] = O("scalar", lambda e, c=ct, sl=slot, db_=db:
                         e.activation(db_[:], xstage[:, sl, :], AF.Square,
                                      accum_out=sqs[:, c:c + 1]), inc=c_act)
            dump_post(c_act, m_sq[ct])
        c0, c1 = 4 * g, 4 * g + 4
        W("vector", c_ve, m_red[c1 - 1])
        W("vector", c_act, m_sq[c1 - 1])
        mv = O("vector", lambda e, a=c0, b=c1: e.tensor_scalar(
            meanc[:, a:b], sums[:, a:b], 1.0 / D, None, ALU.mult), inc=c_ve)
        mv = O("vector", lambda e, a=c0, b=c1: e.tensor_scalar(
            msqc[:, a:b], sqs[:, a:b], 1.0 / D, None, ALU.mult), inc=c_ve)
        W("vector", c_ve, mv)
        mv = O("vector", lambda e, a=c0, b=c1: e.tensor_mul(
            m2c[:, a:b], meanc[:, a:b], meanc[:, a:b]), inc=c_ve)
        W("vector", c_ve, mv)
        mv = O("vector", lambda e, a=c0, b=c1: e.scalar_tensor_tensor(
            vepsc[:, a:b], msqc[:, a:b], 1e-5, m2c[:, a:b],
            ALU.add, ALU.subtract), inc=c_ve)
        W("scalar", c_ve, mv)
        ma = O("scalar", lambda e, a=c0, b=c1: e.activation(
            stdc[:, a:b], vepsc[:, a:b], AF.Sqrt), inc=c_act)
        W("vector", c_act, ma)
        mv = O("vector", lambda e, a=c0, b=c1: e.reciprocal(
            rstdc[:, a:b], stdc[:, a:b]), inc=c_ve)
        W("vector", c_ve, mv)
        W("vector", c_ve, mv)
        for ct in cts:
            slot = ct % 4
            m_apply[ct] = O("vector", lambda e, c=ct, sl=slot: e.tensor_scalar(
                X[:, c, :], xstage[:, sl, :], meanc[:, c:c + 1],
                rstdc[:, c:c + 1], ALU.subtract, ALU.mult), inc=c_ve)
        for ct in cts:
            xeng = "scalar" if ct % 2 == 0 else "vector"
            xsem = c_act if ct % 2 == 0 else c_ve
            W(xeng, c_ve, m_apply[ct])
            if ct >= 1:
                W(xeng, c_pe, m_dmm[ct - 1])
            if ct % 2 == 0:
                m_xsq[ct] = O("scalar", lambda e, c=ct: e.activation(
                    xsqscr[:], X[:, c, :], AF.Square), inc=c_act)
            else:
                m_xsq[ct] = O("vector", lambda e, c=ct: e.tensor_mul(
                    xsqscr[:], X[:, c, :], X[:, c, :]), inc=c_ve)
            W("tensor", xsem, m_xsq[ct])
            if ct == 0:
                W("tensor", c_ve, m_init_ve)
            for s in range(NS):
                last = (s == NS - 1)
                mp = O("tensor", lambda e, ss=s, c=ct: e.matmul(
                    dps[:, ss:ss + 1], xsqscr[:, ss * 128:(ss + 1) * 128],
                    onescolb[:], start=(c == 0 and ss == 0),
                    stop=(c == NT - 1 and ss == NS - 1)),
                    inc=c_pe if last else None)
                if last:
                    m_dmm[ct] = mp
        g1_emit_dma(); g1_emit_abs()

    W("vector", c_pe, m_dmm[NT - 1])
    mv = O("vector", lambda e: e.tensor_copy(dcol[:], dps[:]), inc=c_ve)
    W("vector", c_ve, mv)
    m_ndcol = O("vector", lambda e: e.tensor_scalar(
        ndcol[:], dcol[:], -1.0, None, ALU.mult), inc=c_ve)
    dps_s.close()
    ph0.close()

    if MK_PH < 1:
        return emit_all((ph01,))

    ph1 = ExitStack()
    Pbuf = ph1.enter_context(nc.sbuf_tensor("Pbuf", [128, 2, 2048], BF16))
    atst = ph1.enter_context(nc.sbuf_tensor("atst", [128, 2, 2048], BF16))
    xtst = ph1.enter_context(nc.sbuf_tensor("xtst", [128, 2, 2048], BF16))
    Eps = ph1.enter_context(nc.psum_tensor("Eps", [128, 2048], F32))
    ptp = ph1.enter_context(nc.psum_tensor("ptp", [128, 16, 128], BF16))
    xtp = ph1.enter_context(nc.psum_tensor("xtp", [128, 16, 128], BF16))

    m_exp = {}; m_ptp = {}; m_xttp = {}; m_atev = {}; m_xtev = {}; m_zred = {}
    m_e = {}

    m_exp4 = {}
    for s in range(NS):
        if s == 0:
            W("tensor", c_ve, m_ndcol)
        for n in range(4):
            if s > 0:
                W("tensor", c_act, m_exp4[(s - 1, n)])
            for ct in range(NT):
                last = (ct == NT - 1 and n == 3)
                me = O("tensor", lambda e, c=ct, nn=n, ss=s: e.matmul(
                    Eps[:, nn * 512:(nn + 1) * 512],
                    X[:, c, ss * 128:(ss + 1) * 128],
                    X[:, c, nn * 512:(nn + 1) * 512],
                    start=(c == 0), stop=(c == NT - 1)),
                    inc=c_pe if (ct == NT - 1) else None)
                if ct == NT - 1:
                    m_e4n = me
            m_e[(s, n)] = m_e4n
        if s == 0:
            W("tensor", c_po, m_init_po)
        if s >= 1:
            W("tensor", c_ve, m_xtev[s - 1])
        for ct in range(NTO):
            mx = O("tensor", lambda e, c=ct, ss=s: e.transpose(
                xtp[:, c, :], X[:, c, ss * 128:(ss + 1) * 128], identb[:]),
                inc=c_pe if ct == NTO - 1 else None)
            if ct == NTO - 1:
                m_xttp[s] = mx
        if s == 0:
            W("scalar", c_ve, m_ndcol)
        if s >= 2:
            W("scalar", c_pe, m_ptp[s - 2])
            W("scalar", c_ve, m_zred[s - 2])
        for n in range(4):
            W("scalar", c_pe, m_e[(s, n)])
            m_exp4[(s, n)] = O("scalar", lambda e, ss=s, nn=n: e.activation(
                Pbuf[:, ss % 2, nn * 512:(nn + 1) * 512],
                Eps[:, nn * 512:(nn + 1) * 512], AF.Exp,
                bias=ndcol[:, ss:ss + 1], scale=sone[:]), inc=c_act)
        m_exp[s] = m_exp4[(s, 3)]
        W("vector", c_act, m_exp[s])
        m_zred[s] = O("vector", lambda e, ss=s: e.reduce_sum(
            zcol[:, ss:ss + 1], Pbuf[:, ss % 2, :], axis=AX.X), inc=c_ve)
        W("tensor", c_act, m_exp[s])
        if s >= 1:
            W("tensor", c_act, m_atev[s - 1])
        for t in range(16):
            mp = O("tensor", lambda e, tt=t, ss=s: e.transpose(
                ptp[:, tt, :], Pbuf[:, ss % 2, tt * 128:(tt + 1) * 128],
                identb[:]), inc=c_pe if t == 15 else None)
            if t == 15:
                m_ptp[s] = mp
        W("scalar", c_pe, m_ptp[s])
        if s >= 2:
            ch_pdw.wait("scalar", s % 2)
        m_atev[s] = O("scalar", lambda e, ss=s: e.copy(
            atst[:, ss % 2, :], ptp.rearrange("p a b -> p (a b)")), inc=c_act)
        W("vector", c_pe, m_xttp[s])
        if s >= 2:
            ch_xtw.wait("vector", s % 2)
        m_xtev[s] = O("vector", lambda e, ss=s: e.tensor_copy(
            xtst[:, ss % 2, :], xtp.rearrange("p a b -> p (a b)")), inc=c_ve)
        W("scalar", c_act, m_atev[s])
        ch_pdw.dma("scalar", s % 2, lambda e, ss=s: e.dma_start(
            pd[ss].rearrange("t jj ii -> jj t ii"),
            atst[:, ss % 2, :].rearrange("p (t i) -> p t i", t=16)))
        W("scalar", c_ve, m_xtev[s])
        ch_xtw.dma("scalar", s % 2, lambda e, ss=s: e.dma_start(
            xtd[ss], xtst[:, ss % 2, :]))
        for _ in range(6):
            g1_emit_dma()
            g1_emit_abs()

    W("vector", c_ve, m_zred[NS - 1])
    m_rz = O("vector", lambda e: e.reciprocal(rzcol[:], zcol[:]), inc=c_ve)
    ph1.close()
    ph01.close()

    if MK_PH < 2:
        return emit_all(())

    ph2 = ExitStack()
    hT = ph2.enter_context(nc.sbuf_tensor("hT", [128, NH, CH], BF16))

    m_os = {}; m_sTev = {}; m_sg = {}; m_hmul = {}; m_x1 = {}; m_x2 = {}
    m_q1 = {}; m_q2 = {}; m_sgnq = {}; m_o3 = {}; m_g = {}; m_oev = {}
    m_q3add = {}; m_sgn3 = {}
    m_gamma_done = None

    for k in range(NK if MK_PH >= 5 else 1):
        ph3 = ExitStack()
        XTc = ph3.enter_context(nc.sbuf_tensor(f"XTc{k}", [128, NS, CH], BF16))
        ph3i = ExitStack()
        PIbuf = ph3i.enter_context(
            nc.sbuf_tensor(f"PIbuf{k}", [128, 2, 16, 128], BF16))
        sTst = ph3i.enter_context(nc.sbuf_tensor(f"sTst{k}", [128, 2, CH], F32))
        osps = ph3i.enter_context(nc.psum_tensor(f"osps{k}", [128, 2, CH], F32))
        gps = ph3i.enter_context(nc.psum_tensor(f"gps{k}", [128, 4], F32))

        for cch in (ch_pdw, ch_xtw, ch_out, ch_s3, ch_q3, ch_w3, ch_str,
                    ch_xtb, ch_stw):
            cch.wait_all("scalar")
        W("scalar", c_pe, c_pe.n)
        W("scalar", c_act, c_act.n)
        W("scalar", c_ve, c_ve.n)
        for s in range(NS):
            ch_xtc.dma("scalar", 0, lambda e, ss=s, kk=k, T=XTc: e.dma_start(
                T[:, ss, :], xtd[ss][:, kk * CH:(kk + 1) * CH]))
        xtc_sem, xtc_val = ch_xtc.sems[0], ch_xtc.sems[0].n

        ch_pdr.dma("scalar", 0, lambda e, T=PIbuf: e.dma_start(
            T[:, 0, :, :], pd[0].rearrange("t jj ii -> jj t ii")))
        for I in range(NS):
            if I + 1 < NS:
                if I >= 1:
                    W("scalar", c_pe, m_os[(k, I - 1)])
                ch_pdr.dma("scalar", (I + 1) % 2, lambda e, II=I + 1, T=PIbuf:
                           e.dma_start(
                    T[:, II % 2, :, :],
                    pd[II].rearrange("t jj ii -> jj t ii")))
            ch_pdr.wait("tensor", I % 2)
            W("tensor", xtc_sem, xtc_val)
            if I >= 2:
                W("tensor", c_ve, m_sTev[(k, I - 2)])
            elif k > 0:
                W("tensor", c_act, c_act.n)
            if k == 0 and I == 0:
                W("tensor", c_ve, m_rz)
            for t in range(NS):
                for n in range(2):
                    last = (t == NS - 1 and n == 1)
                    mo = O("tensor", lambda e, tt=t, nn=n, II=I, Ta=osps,
                           Tb=PIbuf, Tc=XTc: e.matmul(
                        Ta[:, II % 2, nn * 512:(nn + 1) * 512],
                        Tb[:, II % 2, tt, :],
                        Tc[:, tt, nn * 512:(nn + 1) * 512],
                        start=(tt == 0), stop=(tt == NS - 1)),
                        inc=c_pe if last else None)
                    if last:
                        m_os[(k, I)] = mo
            W("vector", c_pe, m_os[(k, I)])
            if I >= 2:
                ch_stw.wait("vector", I % 2)
            m_sTev[(k, I)] = O("vector", lambda e, II=I, Ta=sTst, Tb=osps:
                e.tensor_scalar(
                Ta[:, II % 2, :], Tb[:, II % 2, :],
                rzcol[:, II:II + 1], None, ALU.mult), inc=c_ve)
            W("scalar", c_ve, m_sTev[(k, I)])
            ch_stw.dma("scalar", I % 2, lambda e, II=I, kk=k, T=sTst:
                e.dma_start(
                sTd[II][:, kk * CH:(kk + 1) * CH], T[:, II % 2, :]))
            if k == 0:
                for _ in range(3):
                    g1_emit_dma()
                    g1_emit_abs()

        if k == 0:
            while g1_state["abs"] < G1N:
                g1_emit_dma()
                g1_emit_abs()
            W("vector", c_g1, c_g1.n)
            O("vector", lambda e: e.reduce_sum(
                gsum3[:, 0:1], gcols[:, 0:NH], axis=AX.X), inc=c_ve)
            O("vector", lambda e: e.reduce_sum(
                gsum3[:, 1:2], gcols[:, NH:2 * NH], axis=AX.X), inc=c_ve)
            mv = O("vector", lambda e: e.reduce_sum(
                gsum3[:, 2:3], gcols[:, 2 * NH:2 * NH + 3 * NS], axis=AX.X),
                inc=c_ve)
            W("tensor", c_ve, mv)
            mp = O("tensor", lambda e, T=gps: e.matmul(
                T[:, 0:3], ones128f[:], gsum3[:, 0:3], start=True, stop=True),
                inc=c_pe)
            W("vector", c_pe, mp)
            mv = O("vector", lambda e, T=gps: e.tensor_copy(
                grep3[:, 0:3], T[:, 0:3]), inc=c_ve)
            W("vector", c_ve, mv)
            mv = O("vector", lambda e: e.tensor_scalar(
                t1g[:, 0:3], grep3[:, 0:3], 1.0 / NW, 1e-8, ALU.mult, ALU.add),
                inc=c_ve)
            W("vector", c_ve, mv)
            O("vector", lambda e: e.tensor_scalar(
                tpos3[:, 0:3], t1g[:, 0:3], 0.5, None, ALU.mult), inc=c_ve)
            m_gamma_done = O("vector", lambda e: e.tensor_scalar(
                tneg3[:, 0:3], t1g[:, 0:3], -0.5, None, ALU.mult), inc=c_ve)

        ph3i.close()
        if MK_PH < 3:
            ph3.close()
            break

        ph4a = ExitStack()
        Q1s = ph4a.enter_context(nc.sbuf_tensor(f"Q1s{k}", [128, 2, 16, 128], BF16))
        Q2s = ph4a.enter_context(nc.sbuf_tensor(f"Q2s{k}", [128, 2, 16, 128], BF16))
        s1st = ph4a.enter_context(nc.sbuf_tensor(f"s1st{k}", [128, 2, 2048], BF16))
        s2st = ph4a.enter_context(nc.sbuf_tensor(f"s2st{k}", [128, 2, 2048], BF16))
        sgst = ph4a.enter_context(nc.sbuf_tensor(f"sgst{k}", [128, 2, CH], F32))
        x1ps = ph4a.enter_context(nc.psum_tensor(f"x1ps{k}", [128, 2, CH], F32))
        x2ps = ph4a.enter_context(nc.psum_tensor(f"x2ps{k}", [128, 2, CH], F32))

        pe_after_ph3 = c_pe.n
        ve_after_ph3 = c_ve.n
        act_after_ph3 = c_act.n
        W("sync", c_pe, c_pe.n)
        W("sync", c_ve, c_ve.n)
        W("sync", c_act, c_act.n)
        ch_out.wait_all("sync")
        ch_str.wait_all("sync")
        ch_xtb.wait_all("sync")
        ch_stw.wait_all("sync")
        ch_stw.wait_all("vector")
        ch_pdr.wait_all("vector")
        W("vector", c_pe, pe_after_ph3)
        ch_stw.wait_all("scalar")
        ch_pdr.wait_all("scalar")

        for h in range(NH):
            slot = h % 2
            if k == 0:
                wsl1, wsl2 = (2 * h) % 4, (2 * h + 1) % 4
                if h >= 2:
                    W("sync", c_act, m_sgnq[(h - 2, 1)])
                    W("sync", c_ve, m_sgnq[(h - 2, 2)])
                ch_wg.dma("sync", wsl1, lambda e, hh=h, sl=wsl1: e.dma_start(
                    wg[:, sl, :], w1t[hh].rearrange("p t i -> p (t i)")))
                ch_wg.dma("sync", wsl2, lambda e, hh=h, sl=wsl2: e.dma_start(
                    wg[:, sl, :], w2t[hh].rearrange("p t i -> p (t i)")))
                ch_wg.wait("scalar", wsl1)
                if h == 0:
                    W("scalar", c_ve, m_gamma_done)
                    W("scalar", c_pe, pe_after_ph3)
                if h >= 2:
                    W("scalar", c_ve, m_q1[(k, h - 2)])
                    W("scalar", c_ve, m_q2[(k, h - 2)])
                O("scalar", lambda e, sl=wsl1, so=slot,
                    T=s1st: e.sign(
                    T[:, so, :], wg[:, sl, :], bias=tneg3[:, 0:1]),
                    inc=c_act)
                ma = O("scalar", lambda e, sl=wsl1, so=slot, T=s2st: e.sign(
                    T[:, so, :], wg[:, sl, :], bias=tpos3[:, 0:1]),
                    inc=c_act)
                m_sgnq[(h, 1)] = ma
                W("vector", c_act, ma)
                if h >= 2:
                    W("vector", c_pe, m_x1[(k, h - 2)])
                    ch_s1.wait("vector", slot)
                m_q1[(k, h)] = O("vector", lambda e, so=slot, Ta=Q1s, Tb=s1st,
                    Tc=s2st: e.tensor_add(
                    Ta[:, so, :, :].rearrange("p a b -> p (a b)"),
                    Tb[:, so, :], Tc[:, so, :]), inc=c_ve)
                ch_wg.wait("vector", wsl2)
                W("vector", c_ve, m_q1[(k, h)])
                O("vector", lambda e, sl=wsl2, so=slot, T=s1st: e.tensor_scalar(
                    T[:, so, :], wg[:, sl, :], tpos3[:, 1:2], None, ALU.is_gt),
                    inc=c_ve)
                mq2m = O("vector", lambda e, sl=wsl2, so=slot, T=s2st:
                    e.tensor_scalar(
                    T[:, so, :], wg[:, sl, :], tneg3[:, 1:2], None, ALU.is_lt),
                    inc=c_ve)
                m_sgnq[(h, 2)] = mq2m
                W("vector", c_ve, mq2m)
                if h >= 2:
                    W("vector", c_pe, m_x2[(k, h - 2)])
                    ch_s2.wait("vector", slot)
                m_q2[(k, h)] = O("vector", lambda e, so=slot, Ta=Q2s, Tb=s1st,
                    Tc=s2st: e.tensor_sub(
                    Ta[:, so, :, :].rearrange("p a b -> p (a b)"),
                    Tb[:, so, :], Tc[:, so, :]), inc=c_ve)
                W("scalar", c_ve, m_q1[(k, h)])
                ch_s1.dma("scalar", slot, lambda e, hh=h, so=slot, T=Q1s:
                    e.dma_start(
                    qd1[hh].rearrange("p t i -> p (t i)"),
                    T[:, so, :, :].rearrange("p a b -> p (a b)")))
                W("scalar", c_ve, m_q2[(k, h)])
                ch_s2.dma("scalar", slot, lambda e, hh=h, so=slot, T=Q2s:
                    e.dma_start(
                    qd2[hh].rearrange("p t i -> p (t i)"),
                    T[:, so, :, :].rearrange("p a b -> p (a b)")))
            else:
                for hh2 in ([0, 1] if h == 0 else
                            [h + 1] if h + 1 < NH else []):
                    so2 = hh2 % 2
                    if hh2 >= 2:
                        W("sync", c_pe, m_x2[(k, hh2 - 2)])
                    ch_q1.dma("sync", so2, lambda e, hh=hh2, so=so2, T=Q1s:
                        e.dma_start(
                        T[:, so, :, :].rearrange("p a b -> p (a b)"),
                        qd1[hh].rearrange("p t i -> p (t i)")))
                    ch_q2.dma("sync", so2, lambda e, hh=hh2, so=so2, T=Q2s:
                        e.dma_start(
                        T[:, so, :, :].rearrange("p a b -> p (a b)"),
                        qd2[hh].rearrange("p t i -> p (t i)")))
            if k == 0:
                W("tensor", c_ve, m_q1[(k, h)])
            else:
                ch_q1.wait("tensor", slot)
            if h == 0:
                W("tensor", c_pe, pe_after_ph3)
                W("tensor", c_ve, ve_after_ph3)
                W("tensor", c_act, act_after_ph3)
                W("tensor", xtc_sem, xtc_val)
            if h >= 2:
                W("tensor", c_act, m_sg[(k, h - 2)])
            for t in range(NS):
                for n in range(2):
                    last = (t == NS - 1 and n == 1)
                    mm_ = O("tensor", lambda e, tt=t, nn=n, so=slot, Ta=x1ps,
                            Tb=Q1s, Tc=XTc: e.matmul(
                        Ta[:, so, nn * 512:(nn + 1) * 512],
                        Tb[:, so, tt, :],
                        Tc[:, tt, nn * 512:(nn + 1) * 512],
                        start=(tt == 0), stop=(tt == NS - 1)),
                        inc=c_pe if last else None)
                    if last:
                        m_x1[(k, h)] = mm_
            if k == 0:
                W("tensor", c_ve, m_q2[(k, h)])
            else:
                ch_q2.wait("tensor", slot)
            if h >= 2:
                W("tensor", c_ve, m_hmul[(k, h - 2)])
            for t in range(NS):
                for n in range(2):
                    last = (t == NS - 1 and n == 1)
                    mm_ = O("tensor", lambda e, tt=t, nn=n, so=slot, Ta=x2ps,
                            Tb=Q2s, Tc=XTc: e.matmul(
                        Ta[:, so, nn * 512:(nn + 1) * 512],
                        Tb[:, so, tt, :],
                        Tc[:, tt, nn * 512:(nn + 1) * 512],
                        start=(tt == 0), stop=(tt == NS - 1)),
                        inc=c_pe if last else None)
                    if last:
                        m_x2[(k, h)] = mm_
            for hh in ([h - 1] if h >= 1 else []) + ([h] if h == NH - 1 else []):
                so2 = hh % 2
                W("scalar", c_pe, m_x1[(k, hh)])
                if hh >= 2:
                    W("scalar", c_ve, m_hmul[(k, hh - 2)])
                m_sg[(k, hh)] = O("scalar", lambda e, so=so2, Ta=sgst, Tb=x1ps:
                    e.activation(
                    Ta[:, so, :], Tb[:, so, :], AF.Sigmoid, scale=shalf[:]),
                    inc=c_act)
                W("vector", c_act, m_sg[(k, hh)])
                W("vector", c_pe, m_x2[(k, hh)])
                m_hmul[(k, hh)] = O("vector", lambda e, hh_=hh, so=so2, Ta=sgst,
                    Tb=x2ps: e.tensor_mul(
                    hT[:, hh_, :], Ta[:, so, :], Tb[:, so, :]), inc=c_ve)

        ph4a.close()
        ph3.close()
        if MK_PH < 4:
            break

        ph4b = ExitStack()
        Q3s = ph4b.enter_context(nc.sbuf_tensor(f"Q3s{k}", [128, 2, NH, 128], BF16))
        w3st = ph4b.enter_context(nc.sbuf_tensor(f"w3st{k}", [128, 2, 11, 128], F32))
        s3a = ph4b.enter_context(nc.sbuf_tensor(f"s3a{k}", [128, 2, 1408], BF16))
        s3b = ph4b.enter_context(nc.sbuf_tensor(f"s3b{k}", [128, 2, 1408], BF16))
        gst = ph4b.enter_context(nc.sbuf_tensor(f"gst{k}", [128, 2, CH], F32))
        sTs = ph4b.enter_context(nc.sbuf_tensor(f"sTs{k}", [128, 2, CH], F32))
        xts = ph4b.enter_context(nc.sbuf_tensor(f"xts{k}", [128, 2, CH], F32))
        cb1 = ph4b.enter_context(nc.sbuf_tensor(f"cb1{k}", [128, 2, CH], F32))
        o3ps = ph4b.enter_context(nc.psum_tensor(f"o3ps{k}", [128, 2, CH], F32))

        pe_after_4a = c_pe.n
        act_after_4a = c_act.n
        ve_after_4a = c_ve.n
        for eng in ("sync", "scalar", "vector"):
            W(eng, c_pe, c_pe.n)
            W(eng, c_ve, c_ve.n)
            W(eng, c_act, c_act.n)
            ch_s1.wait_all(eng)
            ch_s2.wait_all(eng)
            ch_q1.wait_all(eng)
            ch_q2.wait_all(eng)
        for I in range(NS):
            slot = I % 2
            if k == 0:
                for q in range(4):
                    j = I * 4 + q
                    qs = j % 2
                    lo = q * 11
                    hi = min(NH, lo + 11)
                    wd = (hi - lo) * 128
                    if j >= 2:
                        pi, pq = divmod(j - 2, 4)
                        W("sync", c_ve, m_sgn3[(pi, pq)])
                    ch_w3.dma("sync", qs, lambda e, II=I, a=lo, b=hi, ss=qs,
                              T=w3st: e.dma_start(
                        T[:, ss, 0:b - a, :].rearrange("p a b -> p (a b)"),
                        w3t[II][:, a:b, :].rearrange("p a b -> p (a b)")))
                    ch_w3.wait("vector", qs)
                    if I == 0 and q == 0:
                        W("vector", c_ve, m_gamma_done)
                        W("vector", c_pe, pe_after_4a)
                    if j >= 2:
                        pi, pq = divmod(j - 2, 4)
                        W("vector", c_ve, m_q3add[(pi, pq)])
                    O("vector", lambda e, ss=qs, ww=wd, Ta=s3a, Tb=w3st:
                        e.tensor_scalar(
                        Ta[:, ss, 0:ww],
                        Tb[:, ss, :, :].rearrange("p a b -> p (a b)")[:, 0:ww],
                        tpos3[:, 2:3], None, ALU.is_gt), inc=c_ve)
                    m_sgn3[(I, q)] = O("vector", lambda e, ss=qs, ww=wd,
                        Ta=s3b, Tb=w3st: e.tensor_scalar(
                        Ta[:, ss, 0:ww],
                        Tb[:, ss, :, :].rearrange("p a b -> p (a b)")[:, 0:ww],
                        tneg3[:, 2:3], None, ALU.is_lt), inc=c_ve)
                    W("vector", c_ve, m_sgn3[(I, q)])
                    if I >= 2 and q == 0:
                        W("vector", c_pe, m_o3[(k, I - 2)])
                        ch_s3.wait("vector", slot)
                    m_q3add[(I, q)] = O("vector",
                        lambda e, a=lo, b=hi, ss=qs, so=slot, ww=wd, Ta=Q3s,
                        Tb=s3a, Tc=s3b:
                        e.tensor_sub(
                            Ta[:, so, a:b, :].rearrange("p a b -> p (a b)"),
                            Tb[:, ss, 0:ww], Tc[:, ss, 0:ww]), inc=c_ve)
                W("scalar", c_ve, m_q3add[(I, 3)])
                ch_s3.dma("scalar", slot, lambda e, II=I, so=slot, T=Q3s:
                    e.dma_start(
                    qd3[II].rearrange("p t i -> p (t i)"),
                    T[:, so, :, :].rearrange("p a b -> p (a b)")))
            else:
                for II2 in ([0, 1] if I == 0 else
                            [I + 1] if I + 1 < NS else []):
                    so2 = II2 % 2
                    if II2 >= 2:
                        W("sync", c_pe, m_o3[(k, II2 - 2)])
                    ch_q3.dma("sync", so2, lambda e, II=II2, so=so2, T=Q3s:
                        e.dma_start(
                        T[:, so, :, :].rearrange("p a b -> p (a b)"),
                        qd3[II].rearrange("p t i -> p (t i)")))
            if I == 0:
                W("scalar", c_ve, ve_after_4a)
                W("scalar", c_act, act_after_4a)
                W("scalar", c_pe, pe_after_4a)
            if I >= 2:
                W("scalar", c_ve, m_oev[(k, I - 2)])
                ch_out.wait("scalar", slot)
            ch_stw.wait("scalar", slot)
            ch_str.dma("scalar", slot, lambda e, II=I, kk=k, T=sTs:
                e.dma_start(
                T[:, II % 2, :], sTd[II][:, kk * CH:(kk + 1) * CH]))
            ch_xtb.dma("scalar", slot, lambda e, II=I, kk=k, T=xts:
                e.dma_start(
                T[:, II % 2, :],
                xtb[II * 128:(II + 1) * 128, kk * CH:(kk + 1) * CH]))
            if k == 0:
                W("tensor", c_ve, m_q3add[(I, 3)])
            else:
                ch_q3.wait("tensor", slot)
            if I == 0:
                W("tensor", c_pe, pe_after_4a)
                W("tensor", c_act, act_after_4a)
                W("tensor", c_ve, ve_after_4a)
            if I >= 2:
                W("tensor", c_act, m_g[(k, I - 2)])
            for t in range(NH):
                for n in range(2):
                    last = (t == NH - 1 and n == 1)
                    mm_ = O("tensor", lambda e, tt=t, nn=n, so=slot, Ta=o3ps,
                            Tb=Q3s: e.matmul(
                        Ta[:, so, nn * 512:(nn + 1) * 512],
                        Tb[:, so, tt, :],
                        hT[:, tt, nn * 512:(nn + 1) * 512],
                        start=(tt == 0), stop=(tt == NH - 1)),
                        inc=c_pe if last else None)
                    if last:
                        m_o3[(k, I)] = mm_
            for hh in ([I - 1] if I >= 1 else []) + ([I] if I == NS - 1 else []):
                so2 = hh % 2
                W("scalar", c_pe, m_o3[(k, hh)])
                if hh >= 2:
                    W("scalar", c_ve, m_oev[(k, hh - 2)])
                m_g[(k, hh)] = O("scalar", lambda e, so=so2, Ta=gst, Tb=o3ps:
                    e.activation(
                    Ta[:, so, :], Tb[:, so, :], AF.Silu, scale=sone[:]),
                    inc=c_act)
                W("vector", c_act, m_g[(k, hh)])
                ch_str.wait("vector", so2)
                ch_xtb.wait("vector", so2)
                if hh >= 2:
                    ch_out.wait("vector", so2)
                mv = O("vector", lambda e, so=so2, Ta=cb1, Tb=gst, Tc=sTs:
                    e.tensor_mul(
                    Ta[:, so, :], Tb[:, so, :], Tc[:, so, :]), inc=c_ve)
                W("vector", c_ve, mv)
                m_oev[(k, hh)] = O("vector", lambda e, so=so2, Ta=sTs, Tb=cb1,
                    Tc=xts: e.tensor_add(
                    Ta[:, so, :], Tb[:, so, :], Tc[:, so, :]), inc=c_ve)
                W("scalar", c_ve, m_oev[(k, hh)])
                ch_out.dma("scalar", so2, lambda e, II=hh, kk=k, T=sTs:
                    e.dma_start(
                    outT[II * 128:(II + 1) * 128, kk * CH:(kk + 1) * CH],
                    T[:, II % 2, :]))

        ph4b.close()

    ch_out.wait_all("scalar")
    for c in (ch_s1, ch_s2, ch_s3, ch_q1, ch_q2, ch_q3, ch_w3, ch_wg):
        c.wait_all("sync")

    ph2.close()
    return emit_all(())


def kernel(L0_attn_x, L0_ffn_w1, L0_ffn_w2, L0_ffn_w3):
    x = np.ascontiguousarray(np.asarray(L0_attn_x, dtype=np.float32))
    w1 = np.asarray(L0_ffn_w1, dtype=np.float32)
    w2 = np.asarray(L0_ffn_w2, dtype=np.float32)
    w3 = np.asarray(L0_ffn_w3, dtype=np.float32)

    w1t = np.ascontiguousarray(w1.reshape(NH, 128, 16, 128).transpose(0, 3, 2, 1))
    w2t = np.ascontiguousarray(w2.reshape(NH, 128, 16, 128).transpose(0, 3, 2, 1))
    w3t = np.ascontiguousarray(w3.reshape(NS, 128, NH, 128).transpose(0, 3, 2, 1))

    in_maps = []
    for core in range(8):
        b, p = divmod(core, 2)
        own = x[b, p * S_OWN:(p + 1) * S_OWN]
        oth = x[b, (1 - p) * S_OWN:(2 - p) * S_OWN]
        xbc = np.ascontiguousarray(np.concatenate([own, oth], axis=0))
        xtbc = np.ascontiguousarray(own.T)
        in_maps.append({"xb": xbc, "xtb": xtbc, "w1t": w1t, "w2t": w2t,
                        "w3t": w3t})

    nc = build_program()
    res = run_bass_kernel_spmd(nc, in_maps, core_ids=list(range(8)))

    out = np.empty((4, S_ALL, D), dtype=np.float32)
    for core in range(8):
        b, p = divmod(core, 2)
        out[b, p * S_OWN:(p + 1) * S_OWN, :] = res.results[core]["outT"].T
    return out


# version 14
# speedup vs baseline: 1.0929x; 1.0346x over previous
import os
import numpy as np
from contextlib import ExitStack

import concourse.bass as bass
import concourse.mybir as mybir
from concourse.bass_utils import run_bass_kernel_spmd

MK_PH = int(os.environ.get("MK_PH", "99"))

F32 = mybir.dt.float32
BF16 = mybir.dt.bfloat16
AF = mybir.ActivationFunctionType
ALU = mybir.AluOpType
AX = mybir.AxisListType

D = 2048
S_ALL = 4096
S_OWN = 2048
H = 5632
NT = 32
NTO = 16
NS = 16
NH = 44
CH = 1024
NK = 2
NW = float(H * D)


class Sem:
    def __init__(self, h):
        self.h = h
        self.n = 0


class Rec:
    ENGS = ("sync", "scalar", "vector", "tensor", "gpsimd")

    def __init__(self):
        self.ops = {k: [] for k in self.ENGS}

    def emit(self, eng, f):
        self.ops[eng].append(f)

    def wait(self, eng, sem, v=None):
        v = sem.n if v is None else v
        assert 0 <= v <= sem.n, f"wait beyond emitted count: {v} > {sem.n}"
        if v > 0:
            self.emit(eng, lambda e, s=sem.h, vv=v: e.wait_ge(s, vv))

    def op(self, eng, f, inc=None, k=1):
        if inc is None:
            self.emit(eng, f)
            return None
        inc.n += k
        self.emit(eng, lambda e, ff=f, s=inc.h, kk=k: ff(e).then_inc(s, kk))
        return inc.n


class DmaChan:

    def __init__(self, rec, nc, es, name, nslots):
        self.r = rec
        self.sems = [Sem(es.enter_context(nc.semaphore(f"{name}{i}")))
                     for i in range(nslots)]

    def dma(self, eng, slot, f):
        return self.r.op(eng, f, inc=self.sems[slot], k=16)

    def wait(self, eng, slot, v=None):
        self.r.wait(eng, self.sems[slot], v)

    def wait_all(self, eng):
        for s in self.sems:
            self.r.wait(eng, s)


def build_program():
    nc = bass.Bass("TRN2", target_bir_lowering=False)

    xb = nc.dram_tensor("xb", [S_ALL, D], F32, kind="ExternalInput")
    xtb = nc.dram_tensor("xtb", [D, S_OWN], F32, kind="ExternalInput")
    w1t = nc.dram_tensor("w1t", [NH, 128, 16, 128], F32, kind="ExternalInput")
    w2t = nc.dram_tensor("w2t", [NH, 128, 16, 128], F32, kind="ExternalInput")
    w3t = nc.dram_tensor("w3t", [NS, 128, NH, 128], F32, kind="ExternalInput")
    outT = nc.dram_tensor("outT", [D, S_OWN], F32, kind="ExternalOutput")

    xtd = nc.dram_tensor("xtd", [NS, 128, S_OWN], BF16, kind="Internal")
    pd = nc.dram_tensor("pd", [NS, 16, 128, 128], BF16, kind="Internal")
    sTd = nc.dram_tensor("sTd", [NS, 128, S_OWN], F32, kind="Internal")
    qd1 = nc.dram_tensor("qd1", [NH, 128, 16, 128], BF16, kind="Internal")
    qd2 = nc.dram_tensor("qd2", [NH, 128, 16, 128], BF16, kind="Internal")
    qd3 = nc.dram_tensor("qd3", [NS, 128, NH, 128], BF16, kind="Internal")

    es = ExitStack()
    mksem = lambda n: Sem(es.enter_context(nc.semaphore(n)))

    wg = es.enter_context(nc.sbuf_tensor("wg", [128, 4, 2048], F32))
    sums = es.enter_context(nc.sbuf_tensor("sums", [128, NT], F32))
    sqs = es.enter_context(nc.sbuf_tensor("sqs", [128, NT], F32))
    meanc = es.enter_context(nc.sbuf_tensor("meanc", [128, NT], F32))
    msqc = es.enter_context(nc.sbuf_tensor("msqc", [128, NT], F32))
    m2c = es.enter_context(nc.sbuf_tensor("m2c", [128, NT], F32))
    vepsc = es.enter_context(nc.sbuf_tensor("vepsc", [128, NT], F32))
    stdc = es.enter_context(nc.sbuf_tensor("stdc", [128, NT], F32))
    rstdc = es.enter_context(nc.sbuf_tensor("rstdc", [128, NT], F32))
    zcol = es.enter_context(nc.sbuf_tensor("zcol", [128, NS], F32))
    rzcol = es.enter_context(nc.sbuf_tensor("rzcol", [128, NS], F32))
    dcol = es.enter_context(nc.sbuf_tensor("dcol", [128, NS], F32))
    ndcol = es.enter_context(nc.sbuf_tensor("ndcol", [128, NS], F32))
    gcols = es.enter_context(nc.sbuf_tensor("gcols", [128, 2 * NH + 3 * NS], F32))
    gsum3 = es.enter_context(nc.sbuf_tensor("gsum3", [128, 4], F32))
    grep3 = es.enter_context(nc.sbuf_tensor("grep3", [128, 4], F32))
    t1g = es.enter_context(nc.sbuf_tensor("t1g", [128, 4], F32))
    tpos3 = es.enter_context(nc.sbuf_tensor("tpos3", [128, 4], F32))
    tneg3 = es.enter_context(nc.sbuf_tensor("tneg3", [128, 4], F32))
    onescolb = es.enter_context(nc.sbuf_tensor("onescolb", [128, 1], BF16))
    ones128f = es.enter_context(nc.sbuf_tensor("ones128f", [128, 128], F32))
    identb = es.enter_context(nc.sbuf_tensor("identb", [128, 128], BF16))
    sone = es.enter_context(nc.sbuf_tensor("sone", [128, 1], F32))
    shalf = es.enter_context(nc.sbuf_tensor("shalf", [128, 1], F32))
    squart = es.enter_context(nc.sbuf_tensor("squart", [128, 1], F32))
    g1dump = es.enter_context(nc.sbuf_tensor("g1dump", [128, 2048], F32))

    _dmp = {"i": 0, "marks": [None, None]}

    c_pe = mksem("c_pe")
    c_act = mksem("c_act")
    c_ve = mksem("c_ve")
    c_po = mksem("c_po")
    c_g1 = mksem("c_g1")

    r = Rec()
    W, O = r.wait, r.op
    ch = lambda name, n: DmaChan(r, nc, es, name, n)

    def dump_pre():
        if _dmp["marks"][0] is not None:
            sem, v = _dmp["marks"][0]
            W("scalar", sem, v)
        return g1dump

    def dump_post(sem, val):
        _dmp["marks"][0] = (sem, val)

    all_chans = []

    def ch2(name, n):
        c = ch(name, n)
        all_chans.append(c)
        return c

    ch_x = ch2("dx", 4)
    ch_wg = ch2("dwg", 4)
    ch_pdw = ch2("dpw", 2)
    ch_xtw = ch2("dxw", 2)
    ch_pdr = ch2("dpr", 2)
    ch_xtc = ch2("dxc", 1)
    ch_stw = ch2("dsw", 2)
    ch_str = ch2("dsr", 3)
    ch_xtb = ch2("dxb", 3)
    ch_out = ch2("dou", 3)
    ch_q1 = ch2("dq1", 2)
    ch_q2 = ch2("dq2", 2)
    ch_q3 = ch2("dq3", 2)
    ch_w3 = ch2("dw3", 2)
    ch_s1 = ch2("ds1", 2)
    ch_s2 = ch2("ds2", 2)
    ch_s3 = ch2("ds3", 2)

    O("vector", lambda e: e.memset(onescolb[:], 1.0))
    O("vector", lambda e: e.memset(ones128f[:], 1.0))
    O("vector", lambda e: e.memset(sone[:], 1.0))
    O("vector", lambda e: e.memset(shalf[:], 0.5))
    m_init_ve = O("vector", lambda e: e.memset(squart[:], 0.25), inc=c_ve)
    mpo = O("gpsimd", lambda e: e.memset(identb[:], 0.0), inc=c_po)
    W("gpsimd", c_po, mpo)
    m_init_po = O("gpsimd", lambda e: e.affine_select(
        out=identb[:], in_=identb[:], compare_op=ALU.not_equal, fill=1.0,
        base=0, pattern=[[-1, 128]], channel_multiplier=1), inc=c_po)

    g1_chunks = ([("w1", h, 0) for h in range(NH)]
                 + [("w2", h, 0) for h in range(NH)]
                 + [("w3", i, pp) for i in range(NS) for pp in range(3)])
    G1N = len(g1_chunks)
    g1_state = {"dma": 0, "abs": 0}
    g1_marks = {}

    def g1_emit_dma():
        i = g1_state["dma"]
        if i >= G1N:
            return
        kind, idx, piece = g1_chunks[i]
        slot = i % 4
        if i >= 4:
            W("sync", c_g1, g1_marks[i - 4])
        if kind == "w1":
            ch_wg.dma("sync", slot, lambda e, h=idx, sl=slot: e.dma_start(
                wg[:, sl, :], w1t[h].rearrange("p t i -> p (t i)")))
        elif kind == "w2":
            ch_wg.dma("sync", slot, lambda e, h=idx, sl=slot: e.dma_start(
                wg[:, sl, :], w2t[h].rearrange("p t i -> p (t i)")))
        else:
            lo = piece * 2048
            hi = min(NH * 128, lo + 2048)
            ch_wg.dma("sync", slot, lambda e, I=idx, a=lo, b=hi, sl=slot:
                      e.dma_start(wg[:, sl, 0:b - a],
                                  w3t[I].rearrange("p t i -> p (t i)")[:, a:b]))
        g1_state["dma"] = i + 1

    def g1_emit_abs():
        i = g1_state["abs"]
        if i >= G1N or i >= g1_state["dma"]:
            return
        kind, idx, piece = g1_chunks[i]
        slot = i % 4
        if kind == "w1":
            col, width = idx, 2048
        elif kind == "w2":
            col, width = NH + idx, 2048
        else:
            col = 2 * NH + idx * 3 + piece
            width = 2048 if piece < 2 else 1536
        ch_wg.wait("scalar", slot)
        db = dump_pre()
        g1_marks[i] = O("scalar", lambda e, sl=slot, ww=width, cc=col,
                        db_=db: e.activation(
                            db_[:, 0:ww], wg[:, sl, 0:ww], AF.Abs,
                            accum_out=gcols[:, cc:cc + 1]), inc=c_g1)
        dump_post(c_g1, g1_marks[i])
        g1_state["abs"] = i + 1

    def emit_all(ph_stacks=()):
        for st in ph_stacks:
            st.close()
        for c in all_chans:
            c.wait_all("scalar")
            c.wait_all("sync")
        with es, nc.Block() as blk:
            blk.sync(lambda e: [f(e) for f in r.ops["sync"]])
            blk.scalar(lambda e: [f(e) for f in r.ops["scalar"]])
            blk.vector(lambda e: [f(e) for f in r.ops["vector"]])
            blk.tensor(lambda e: [f(e) for f in r.ops["tensor"]])
            blk.gpsimd(lambda e: [f(e) for f in r.ops["gpsimd"]])
        return nc

    ph01 = ExitStack()
    X = ph01.enter_context(nc.sbuf_tensor("X", [128, NT, 2048], BF16))
    ph0 = ExitStack()
    xstage = ph0.enter_context(nc.sbuf_tensor("xstage", [128, 4, 2048], F32))
    xsqscr = ph0.enter_context(nc.sbuf_tensor("xsqscr", [128, 2048], BF16))
    dps_s = ExitStack()
    dps = dps_s.enter_context(nc.psum_tensor("dps", [128, NS], F32))

    m_apply = {}
    m_sq = {}
    m_dmm = {}
    m_xsq = {}
    m_red = {}

    for g in range(NT // 4):
        cts = list(range(4 * g, 4 * g + 4))
        for ct in cts:
            slot = ct % 4
            q = "scalar" if ct % 2 == 0 else "sync"
            if ct >= 4:
                W(q, c_ve, m_apply[ct - 4])
            ch_x.dma(q, slot, lambda e, c=ct, sl=slot: e.dma_start(
                xstage[:, sl, :], xb[c * 128:(c + 1) * 128, :]))
        g1_emit_dma(); g1_emit_abs()
        g1_emit_dma(); g1_emit_abs()
        for ct in cts:
            slot = ct % 4
            ch_x.wait("vector", slot)
            m_red[ct] = O("vector", lambda e, c=ct, sl=slot: e.reduce_sum(
                sums[:, c:c + 1], xstage[:, sl, :], axis=AX.X), inc=c_ve)
            ch_x.wait("scalar", slot)
            db = dump_pre()
            m_sq[ct] = O("scalar", lambda e, c=ct, sl=slot, db_=db:
                         e.activation(db_[:], xstage[:, sl, :], AF.Square,
                                      accum_out=sqs[:, c:c + 1]), inc=c_act)
            dump_post(c_act, m_sq[ct])
        c0, c1 = 4 * g, 4 * g + 4
        W("vector", c_ve, m_red[c1 - 1])
        W("vector", c_act, m_sq[c1 - 1])
        mv = O("vector", lambda e, a=c0, b=c1: e.tensor_scalar(
            meanc[:, a:b], sums[:, a:b], 1.0 / D, None, ALU.mult), inc=c_ve)
        mv = O("vector", lambda e, a=c0, b=c1: e.tensor_scalar(
            msqc[:, a:b], sqs[:, a:b], 1.0 / D, None, ALU.mult), inc=c_ve)
        W("vector", c_ve, mv)
        mv = O("vector", lambda e, a=c0, b=c1: e.tensor_mul(
            m2c[:, a:b], meanc[:, a:b], meanc[:, a:b]), inc=c_ve)
        W("vector", c_ve, mv)
        mv = O("vector", lambda e, a=c0, b=c1: e.scalar_tensor_tensor(
            vepsc[:, a:b], msqc[:, a:b], 1e-5, m2c[:, a:b],
            ALU.add, ALU.subtract), inc=c_ve)
        W("scalar", c_ve, mv)
        ma = O("scalar", lambda e, a=c0, b=c1: e.activation(
            stdc[:, a:b], vepsc[:, a:b], AF.Sqrt), inc=c_act)
        W("vector", c_act, ma)
        mv = O("vector", lambda e, a=c0, b=c1: e.reciprocal(
            rstdc[:, a:b], stdc[:, a:b]), inc=c_ve)
        W("vector", c_ve, mv)
        W("vector", c_ve, mv)
        for ct in cts:
            slot = ct % 4
            m_apply[ct] = O("vector", lambda e, c=ct, sl=slot: e.tensor_scalar(
                X[:, c, :], xstage[:, sl, :], meanc[:, c:c + 1],
                rstdc[:, c:c + 1], ALU.subtract, ALU.mult), inc=c_ve)
        for ct in cts:
            W("scalar", c_ve, m_apply[ct])
            if ct >= 1:
                W("scalar", c_pe, m_dmm[ct - 1])
            m_xsq[ct] = O("scalar", lambda e, c=ct: e.activation(
                xsqscr[:], X[:, c, :], AF.Square), inc=c_act)
            W("tensor", c_act, m_xsq[ct])
            if ct == 0:
                W("tensor", c_ve, m_init_ve)
            for s in range(NS):
                last = (s == NS - 1)
                mp = O("tensor", lambda e, ss=s, c=ct: e.matmul(
                    dps[:, ss:ss + 1], xsqscr[:, ss * 128:(ss + 1) * 128],
                    onescolb[:], start=(c == 0 and ss == 0),
                    stop=(c == NT - 1 and ss == NS - 1)),
                    inc=c_pe if last else None)
                if last:
                    m_dmm[ct] = mp
        g1_emit_dma(); g1_emit_abs()

    W("vector", c_pe, m_dmm[NT - 1])
    mv = O("vector", lambda e: e.tensor_copy(dcol[:], dps[:]), inc=c_ve)
    W("vector", c_ve, mv)
    m_ndcol = O("vector", lambda e: e.tensor_scalar(
        ndcol[:], dcol[:], -1.0, None, ALU.mult), inc=c_ve)
    dps_s.close()
    ph0.close()

    if MK_PH < 1:
        return emit_all((ph01,))

    ph1 = ExitStack()
    Pbuf = ph1.enter_context(nc.sbuf_tensor("Pbuf", [128, 2, 2048], BF16))
    atst = ph1.enter_context(nc.sbuf_tensor("atst", [128, 2, 2048], BF16))
    xtst = ph1.enter_context(nc.sbuf_tensor("xtst", [128, 2, 2048], BF16))
    Eps = ph1.enter_context(nc.psum_tensor("Eps", [128, 2048], F32))
    ptp = ph1.enter_context(nc.psum_tensor("ptp", [128, 16, 128], BF16))
    xtp = ph1.enter_context(nc.psum_tensor("xtp", [128, 16, 128], BF16))

    m_exp = {}; m_ptp = {}; m_xttp = {}; m_atev = {}; m_xtev = {}; m_zred = {}
    m_e = {}

    m_exp4 = {}
    for s in range(NS):
        if s == 0:
            W("tensor", c_ve, m_ndcol)
        for n in range(4):
            if s > 0:
                W("tensor", c_act, m_exp4[(s - 1, n)])
            for ct in range(NT):
                last = (ct == NT - 1 and n == 3)
                me = O("tensor", lambda e, c=ct, nn=n, ss=s: e.matmul(
                    Eps[:, nn * 512:(nn + 1) * 512],
                    X[:, c, ss * 128:(ss + 1) * 128],
                    X[:, c, nn * 512:(nn + 1) * 512],
                    start=(c == 0), stop=(c == NT - 1)),
                    inc=c_pe if (ct == NT - 1) else None)
                if ct == NT - 1:
                    m_e4n = me
            m_e[(s, n)] = m_e4n
        if s == 0:
            W("tensor", c_po, m_init_po)
        if s >= 1:
            W("tensor", c_ve, m_xtev[s - 1])
        for ct in range(NTO):
            mx = O("tensor", lambda e, c=ct, ss=s: e.transpose(
                xtp[:, c, :], X[:, c, ss * 128:(ss + 1) * 128], identb[:]),
                inc=c_pe if ct == NTO - 1 else None)
            if ct == NTO - 1:
                m_xttp[s] = mx
        if s == 0:
            W("scalar", c_ve, m_ndcol)
        if s >= 2:
            W("scalar", c_pe, m_ptp[s - 2])
            W("scalar", c_ve, m_zred[s - 2])
        for n in range(4):
            W("scalar", c_pe, m_e[(s, n)])
            m_exp4[(s, n)] = O("scalar", lambda e, ss=s, nn=n: e.activation(
                Pbuf[:, ss % 2, nn * 512:(nn + 1) * 512],
                Eps[:, nn * 512:(nn + 1) * 512], AF.Exp,
                bias=ndcol[:, ss:ss + 1], scale=sone[:]), inc=c_act)
        m_exp[s] = m_exp4[(s, 3)]
        W("vector", c_act, m_exp[s])
        m_zred[s] = O("vector", lambda e, ss=s: e.reduce_sum(
            zcol[:, ss:ss + 1], Pbuf[:, ss % 2, :], axis=AX.X), inc=c_ve)
        W("tensor", c_act, m_exp[s])
        if s >= 1:
            W("tensor", c_act, m_atev[s - 1])
        for t in range(16):
            mp = O("tensor", lambda e, tt=t, ss=s: e.transpose(
                ptp[:, tt, :], Pbuf[:, ss % 2, tt * 128:(tt + 1) * 128],
                identb[:]), inc=c_pe if t == 15 else None)
            if t == 15:
                m_ptp[s] = mp
        W("scalar", c_pe, m_ptp[s])
        if s >= 2:
            ch_pdw.wait("scalar", s % 2)
        m_atev[s] = O("scalar", lambda e, ss=s: e.copy(
            atst[:, ss % 2, :], ptp.rearrange("p a b -> p (a b)")), inc=c_act)
        W("vector", c_pe, m_xttp[s])
        if s >= 2:
            ch_xtw.wait("vector", s % 2)
        m_xtev[s] = O("vector", lambda e, ss=s: e.tensor_copy(
            xtst[:, ss % 2, :], xtp.rearrange("p a b -> p (a b)")), inc=c_ve)
        W("scalar", c_act, m_atev[s])
        ch_pdw.dma("scalar", s % 2, lambda e, ss=s: e.dma_start(
            pd[ss].rearrange("t jj ii -> jj t ii"),
            atst[:, ss % 2, :].rearrange("p (t i) -> p t i", t=16)))
        W("scalar", c_ve, m_xtev[s])
        ch_xtw.dma("scalar", s % 2, lambda e, ss=s: e.dma_start(
            xtd[ss], xtst[:, ss % 2, :]))
        for _ in range(6):
            g1_emit_dma()
            g1_emit_abs()

    W("vector", c_ve, m_zred[NS - 1])
    m_rz = O("vector", lambda e: e.reciprocal(rzcol[:], zcol[:]), inc=c_ve)
    ph1.close()
    ph01.close()

    if MK_PH < 2:
        return emit_all(())

    ph2 = ExitStack()
    hT = ph2.enter_context(nc.sbuf_tensor("hT", [128, NH, CH], BF16))

    m_os = {}; m_sTev = {}; m_sg = {}; m_hmul = {}; m_x1 = {}; m_x2 = {}
    m_q1 = {}; m_q2 = {}; m_sgnq = {}; m_o3 = {}; m_g = {}; m_oev = {}
    m_q3add = {}; m_sgn3 = {}
    m_gamma_done = None

    for k in range(NK if MK_PH >= 5 else 1):
        ph3 = ExitStack()
        XTc = ph3.enter_context(nc.sbuf_tensor(f"XTc{k}", [128, NS, CH], BF16))
        ph3i = ExitStack()
        PIbuf = ph3i.enter_context(
            nc.sbuf_tensor(f"PIbuf{k}", [128, 2, 16, 128], BF16))
        sTst = ph3i.enter_context(nc.sbuf_tensor(f"sTst{k}", [128, 2, CH], F32))
        osps = ph3i.enter_context(nc.psum_tensor(f"osps{k}", [128, 2, CH], F32))
        gps = ph3i.enter_context(nc.psum_tensor(f"gps{k}", [128, 4], F32))

        for cch in (ch_pdw, ch_xtw, ch_out, ch_s3, ch_q3, ch_w3, ch_str,
                    ch_xtb, ch_stw):
            cch.wait_all("scalar")
        W("scalar", c_pe, c_pe.n)
        W("scalar", c_act, c_act.n)
        W("scalar", c_ve, c_ve.n)
        for s in range(NS):
            ch_xtc.dma("scalar", 0, lambda e, ss=s, kk=k, T=XTc: e.dma_start(
                T[:, ss, :], xtd[ss][:, kk * CH:(kk + 1) * CH]))
        xtc_sem, xtc_val = ch_xtc.sems[0], ch_xtc.sems[0].n

        ch_pdr.dma("scalar", 0, lambda e, T=PIbuf: e.dma_start(
            T[:, 0, :, :], pd[0].rearrange("t jj ii -> jj t ii")))
        for I in range(NS):
            if I + 1 < NS:
                if I >= 1:
                    W("scalar", c_pe, m_os[(k, I - 1)])
                ch_pdr.dma("scalar", (I + 1) % 2, lambda e, II=I + 1, T=PIbuf:
                           e.dma_start(
                    T[:, II % 2, :, :],
                    pd[II].rearrange("t jj ii -> jj t ii")))
            ch_pdr.wait("tensor", I % 2)
            W("tensor", xtc_sem, xtc_val)
            if I >= 2:
                W("tensor", c_ve, m_sTev[(k, I - 2)])
            elif k > 0:
                W("tensor", c_act, c_act.n)
            if k == 0 and I == 0:
                W("tensor", c_ve, m_rz)
            for t in range(NS):
                for n in range(2):
                    last = (t == NS - 1 and n == 1)
                    mo = O("tensor", lambda e, tt=t, nn=n, II=I, Ta=osps,
                           Tb=PIbuf, Tc=XTc: e.matmul(
                        Ta[:, II % 2, nn * 512:(nn + 1) * 512],
                        Tb[:, II % 2, tt, :],
                        Tc[:, tt, nn * 512:(nn + 1) * 512],
                        start=(tt == 0), stop=(tt == NS - 1)),
                        inc=c_pe if last else None)
                    if last:
                        m_os[(k, I)] = mo
            W("vector", c_pe, m_os[(k, I)])
            if I >= 2:
                ch_stw.wait("vector", I % 2)
            m_sTev[(k, I)] = O("vector", lambda e, II=I, Ta=sTst, Tb=osps:
                e.tensor_scalar(
                Ta[:, II % 2, :], Tb[:, II % 2, :],
                rzcol[:, II:II + 1], None, ALU.mult), inc=c_ve)
            W("scalar", c_ve, m_sTev[(k, I)])
            ch_stw.dma("scalar", I % 2, lambda e, II=I, kk=k, T=sTst:
                e.dma_start(
                sTd[II][:, kk * CH:(kk + 1) * CH], T[:, II % 2, :]))
            if k == 0:
                for _ in range(3):
                    g1_emit_dma()
                    g1_emit_abs()

        if k == 0:
            while g1_state["abs"] < G1N:
                g1_emit_dma()
                g1_emit_abs()
            W("vector", c_g1, c_g1.n)
            O("vector", lambda e: e.reduce_sum(
                gsum3[:, 0:1], gcols[:, 0:NH], axis=AX.X), inc=c_ve)
            O("vector", lambda e: e.reduce_sum(
                gsum3[:, 1:2], gcols[:, NH:2 * NH], axis=AX.X), inc=c_ve)
            mv = O("vector", lambda e: e.reduce_sum(
                gsum3[:, 2:3], gcols[:, 2 * NH:2 * NH + 3 * NS], axis=AX.X),
                inc=c_ve)
            W("tensor", c_ve, mv)
            mp = O("tensor", lambda e, T=gps: e.matmul(
                T[:, 0:3], ones128f[:], gsum3[:, 0:3], start=True, stop=True),
                inc=c_pe)
            W("vector", c_pe, mp)
            mv = O("vector", lambda e, T=gps: e.tensor_copy(
                grep3[:, 0:3], T[:, 0:3]), inc=c_ve)
            W("vector", c_ve, mv)
            mv = O("vector", lambda e: e.tensor_scalar(
                t1g[:, 0:3], grep3[:, 0:3], 1.0 / NW, 1e-8, ALU.mult, ALU.add),
                inc=c_ve)
            W("vector", c_ve, mv)
            O("vector", lambda e: e.tensor_scalar(
                tpos3[:, 0:3], t1g[:, 0:3], 0.5, None, ALU.mult), inc=c_ve)
            m_gamma_done = O("vector", lambda e: e.tensor_scalar(
                tneg3[:, 0:3], t1g[:, 0:3], -0.5, None, ALU.mult), inc=c_ve)

        ph3i.close()
        if MK_PH < 3:
            ph3.close()
            break

        ph4a = ExitStack()
        Q1s = ph4a.enter_context(nc.sbuf_tensor(f"Q1s{k}", [128, 2, 16, 128], BF16))
        Q2s = ph4a.enter_context(nc.sbuf_tensor(f"Q2s{k}", [128, 2, 16, 128], BF16))
        s1st = ph4a.enter_context(nc.sbuf_tensor(f"s1st{k}", [128, 2, 2048], BF16))
        s2st = ph4a.enter_context(nc.sbuf_tensor(f"s2st{k}", [128, 2, 2048], BF16))
        sgst = ph4a.enter_context(nc.sbuf_tensor(f"sgst{k}", [128, 2, CH], F32))
        x1ps = ph4a.enter_context(nc.psum_tensor(f"x1ps{k}", [128, 2, CH], F32))
        x2ps = ph4a.enter_context(nc.psum_tensor(f"x2ps{k}", [128, 2, CH], F32))

        pe_after_ph3 = c_pe.n
        ve_after_ph3 = c_ve.n
        act_after_ph3 = c_act.n
        W("sync", c_pe, c_pe.n)
        W("sync", c_ve, c_ve.n)
        W("sync", c_act, c_act.n)
        ch_out.wait_all("sync")
        ch_str.wait_all("sync")
        ch_xtb.wait_all("sync")
        ch_stw.wait_all("sync")
        ch_stw.wait_all("vector")
        ch_pdr.wait_all("vector")
        W("vector", c_pe, pe_after_ph3)
        ch_stw.wait_all("scalar")
        ch_pdr.wait_all("scalar")

        def emit_sg4a(hh):
            so2 = hh % 2
            W("scalar", c_pe, m_x1[(k, hh)])
            if hh >= 2:
                W("scalar", c_ve, m_hmul[(k, hh - 2)])
            m_sg[(k, hh)] = O("scalar", lambda e, so=so2, Ta=sgst, Tb=x1ps:
                e.activation(
                Ta[:, so, :], Tb[:, so, :], AF.Sigmoid, scale=shalf[:]),
                inc=c_act)

        def emit_hm4a(hh):
            so2 = hh % 2
            W("vector", c_act, m_sg[(k, hh)])
            W("vector", c_pe, m_x2[(k, hh)])
            m_hmul[(k, hh)] = O("vector", lambda e, hh_=hh, so=so2, Ta=sgst,
                Tb=x2ps: e.tensor_mul(
                hT[:, hh_, :], Ta[:, so, :], Tb[:, so, :]), inc=c_ve)

        for h in range(NH):
            slot = h % 2
            if k == 0:
                wsl1, wsl2 = (2 * h) % 4, (2 * h + 1) % 4
                if h >= 2:
                    W("sync", c_act, m_sgnq[(h - 2, 1)])
                    W("sync", c_ve, m_sgnq[(h - 2, 2)])
                ch_wg.dma("sync", wsl1, lambda e, hh=h, sl=wsl1: e.dma_start(
                    wg[:, sl, :], w1t[hh].rearrange("p t i -> p (t i)")))
                ch_wg.dma("sync", wsl2, lambda e, hh=h, sl=wsl2: e.dma_start(
                    wg[:, sl, :], w2t[hh].rearrange("p t i -> p (t i)")))
                ch_wg.wait("scalar", wsl1)
                if h == 0:
                    W("scalar", c_ve, m_gamma_done)
                    W("scalar", c_pe, pe_after_ph3)
                if h >= 2:
                    W("scalar", c_ve, m_q1[(k, h - 2)])
                    W("scalar", c_ve, m_q2[(k, h - 2)])
                O("scalar", lambda e, sl=wsl1, so=slot,
                    T=s1st: e.sign(
                    T[:, so, :], wg[:, sl, :], bias=tneg3[:, 0:1]),
                    inc=c_act)
                ma = O("scalar", lambda e, sl=wsl1, so=slot, T=s2st: e.sign(
                    T[:, so, :], wg[:, sl, :], bias=tpos3[:, 0:1]),
                    inc=c_act)
                m_sgnq[(h, 1)] = ma
                W("vector", c_act, ma)
                if h >= 2:
                    W("vector", c_pe, m_x1[(k, h - 2)])
                    ch_s1.wait("vector", slot)
                m_q1[(k, h)] = O("vector", lambda e, so=slot, Ta=Q1s, Tb=s1st,
                    Tc=s2st: e.tensor_add(
                    Ta[:, so, :, :].rearrange("p a b -> p (a b)"),
                    Tb[:, so, :], Tc[:, so, :]), inc=c_ve)
                ch_wg.wait("vector", wsl2)
                W("vector", c_ve, m_q1[(k, h)])
                O("vector", lambda e, sl=wsl2, so=slot, T=s1st: e.tensor_scalar(
                    T[:, so, :], wg[:, sl, :], tpos3[:, 1:2], None, ALU.is_gt),
                    inc=c_ve)
                mq2m = O("vector", lambda e, sl=wsl2, so=slot, T=s2st:
                    e.tensor_scalar(
                    T[:, so, :], wg[:, sl, :], tneg3[:, 1:2], None, ALU.is_lt),
                    inc=c_ve)
                m_sgnq[(h, 2)] = mq2m
                W("vector", c_ve, mq2m)
                if h >= 2:
                    W("vector", c_pe, m_x2[(k, h - 2)])
                    ch_s2.wait("vector", slot)
                m_q2[(k, h)] = O("vector", lambda e, so=slot, Ta=Q2s, Tb=s1st,
                    Tc=s2st: e.tensor_sub(
                    Ta[:, so, :, :].rearrange("p a b -> p (a b)"),
                    Tb[:, so, :], Tc[:, so, :]), inc=c_ve)
                W("scalar", c_ve, m_q1[(k, h)])
                ch_s1.dma("scalar", slot, lambda e, hh=h, so=slot, T=Q1s:
                    e.dma_start(
                    qd1[hh].rearrange("p t i -> p (t i)"),
                    T[:, so, :, :].rearrange("p a b -> p (a b)")))
                W("scalar", c_ve, m_q2[(k, h)])
                ch_s2.dma("scalar", slot, lambda e, hh=h, so=slot, T=Q2s:
                    e.dma_start(
                    qd2[hh].rearrange("p t i -> p (t i)"),
                    T[:, so, :, :].rearrange("p a b -> p (a b)")))
            else:
                for hh2 in ([0, 1] if h == 0 else
                            [h + 1] if h + 1 < NH else []):
                    so2 = hh2 % 2
                    if hh2 >= 2:
                        W("sync", c_pe, m_x2[(k, hh2 - 2)])
                    ch_q1.dma("sync", so2, lambda e, hh=hh2, so=so2, T=Q1s:
                        e.dma_start(
                        T[:, so, :, :].rearrange("p a b -> p (a b)"),
                        qd1[hh].rearrange("p t i -> p (t i)")))
                    ch_q2.dma("sync", so2, lambda e, hh=hh2, so=so2, T=Q2s:
                        e.dma_start(
                        T[:, so, :, :].rearrange("p a b -> p (a b)"),
                        qd2[hh].rearrange("p t i -> p (t i)")))
            if h >= 1:
                emit_sg4a(h - 1)
            if h >= 2:
                emit_hm4a(h - 2)
            if k == 0:
                W("tensor", c_ve, m_q1[(k, h)])
            else:
                ch_q1.wait("tensor", slot)
            if h == 0:
                W("tensor", c_pe, pe_after_ph3)
                W("tensor", c_ve, ve_after_ph3)
                W("tensor", c_act, act_after_ph3)
                W("tensor", xtc_sem, xtc_val)
            if h >= 2:
                W("tensor", c_act, m_sg[(k, h - 2)])
            for t in range(NS):
                for n in range(2):
                    last = (t == NS - 1 and n == 1)
                    mm_ = O("tensor", lambda e, tt=t, nn=n, so=slot, Ta=x1ps,
                            Tb=Q1s, Tc=XTc: e.matmul(
                        Ta[:, so, nn * 512:(nn + 1) * 512],
                        Tb[:, so, tt, :],
                        Tc[:, tt, nn * 512:(nn + 1) * 512],
                        start=(tt == 0), stop=(tt == NS - 1)),
                        inc=c_pe if last else None)
                    if last:
                        m_x1[(k, h)] = mm_
            if k == 0:
                W("tensor", c_ve, m_q2[(k, h)])
            else:
                ch_q2.wait("tensor", slot)
            if h >= 2:
                W("tensor", c_ve, m_hmul[(k, h - 2)])
            for t in range(NS):
                for n in range(2):
                    last = (t == NS - 1 and n == 1)
                    mm_ = O("tensor", lambda e, tt=t, nn=n, so=slot, Ta=x2ps,
                            Tb=Q2s, Tc=XTc: e.matmul(
                        Ta[:, so, nn * 512:(nn + 1) * 512],
                        Tb[:, so, tt, :],
                        Tc[:, tt, nn * 512:(nn + 1) * 512],
                        start=(tt == 0), stop=(tt == NS - 1)),
                        inc=c_pe if last else None)
                    if last:
                        m_x2[(k, h)] = mm_

        for _hh in (NH - 1,):
            emit_sg4a(_hh)
        for _hh in (NH - 2, NH - 1):
            emit_hm4a(_hh)


        ph4a.close()
        ph3.close()
        if MK_PH < 4:
            break

        ph4b = ExitStack()
        Q3s = ph4b.enter_context(nc.sbuf_tensor(f"Q3s{k}", [128, 2, NH, 128], BF16))
        w3st = ph4b.enter_context(nc.sbuf_tensor(f"w3st{k}", [128, 2, 11, 128], F32))
        s3a = ph4b.enter_context(nc.sbuf_tensor(f"s3a{k}", [128, 1, 1408], BF16))
        s3b = ph4b.enter_context(nc.sbuf_tensor(f"s3b{k}", [128, 1, 1408], BF16))
        gst = ph4b.enter_context(nc.sbuf_tensor(f"gst{k}", [128, 2, CH], F32))
        sTs = ph4b.enter_context(nc.sbuf_tensor(f"sTs{k}", [128, 3, CH], F32))
        xts = ph4b.enter_context(nc.sbuf_tensor(f"xts{k}", [128, 3, CH], F32))
        cb1 = ph4b.enter_context(nc.sbuf_tensor(f"cb1{k}", [128, 1, CH], F32))
        o3ps = ph4b.enter_context(nc.psum_tensor(f"o3ps{k}", [128, 2, CH], F32))

        pe_after_4a = c_pe.n
        act_after_4a = c_act.n
        ve_after_4a = c_ve.n

        def emit_silu4b(hh):
            so2 = hh % 2
            W("scalar", c_pe, m_o3[(k, hh)])
            if hh >= 3:
                W("scalar", c_ve, m_oev[(k, hh - 3)])
            m_g[(k, hh)] = O("scalar", lambda e, so=so2, Ta=gst, Tb=o3ps:
                e.activation(
                Ta[:, so, :], Tb[:, so, :], AF.Silu, scale=sone[:]),
                inc=c_act)

        def emit_comb4b(hh):
            so2 = hh % 2
            s3 = hh % 3
            W("vector", c_act, m_g[(k, hh)])
            ch_str.wait("vector", s3)
            ch_xtb.wait("vector", s3)
            if hh >= 1:
                W("vector", c_ve, m_oev[(k, hh - 1)])
            mv = O("vector", lambda e, so=so2, s3_=s3, Ta=cb1, Tb=gst,
                Tc=sTs: e.tensor_mul(
                Ta[:, 0, :], Tb[:, so, :], Tc[:, s3_, :]), inc=c_ve)
            W("vector", c_ve, mv)
            m_oev[(k, hh)] = O("vector", lambda e, so=so2, s3_=s3, Ta=sTs,
                Tb=cb1, Tc=xts: e.tensor_add(
                Ta[:, s3_, :], Tb[:, 0, :], Tc[:, s3_, :]), inc=c_ve)
            W("scalar", c_ve, m_oev[(k, hh)])
            ch_out.dma("scalar", s3, lambda e, II=hh, kk=k, T=sTs, s3_=s3:
                e.dma_start(
                outT[II * 128:(II + 1) * 128, kk * CH:(kk + 1) * CH],
                T[:, s3_, :]))

        for eng in ("sync", "scalar", "vector"):
            W(eng, c_pe, c_pe.n)
            W(eng, c_ve, c_ve.n)
            W(eng, c_act, c_act.n)
            ch_s1.wait_all(eng)
            ch_s2.wait_all(eng)
            ch_q1.wait_all(eng)
            ch_q2.wait_all(eng)
        for I in range(NS):
            slot = I % 2
            if k == 0:
                for q in range(4):
                    j = I * 4 + q
                    qs = 0
                    if j >= 1:
                        pi, pq = divmod(j - 1, 4)
                        W("vector", c_ve, m_q3add[(pi, pq)])
                    lo = q * 11
                    hi = min(NH, lo + 11)
                    wd = (hi - lo) * 128
                    wsl = j % 2
                    if j >= 2:
                        pi, pq = divmod(j - 2, 4)
                        W("sync", c_ve, m_sgn3[(pi, pq)])
                    ch_w3.dma("sync", wsl, lambda e, II=I, a=lo, b=hi, ss=wsl,
                              T=w3st: e.dma_start(
                        T[:, ss, 0:b - a, :].rearrange("p a b -> p (a b)"),
                        w3t[II][:, a:b, :].rearrange("p a b -> p (a b)")))
                    ch_w3.wait("vector", wsl)
                    if I == 0 and q == 0:
                        W("vector", c_ve, m_gamma_done)
                        W("vector", c_pe, pe_after_4a)
                    if j >= 2:
                        pi, pq = divmod(j - 2, 4)
                        W("vector", c_ve, m_q3add[(pi, pq)])
                    O("vector", lambda e, ss=wsl, ww=wd, Ta=s3a, Tb=w3st:
                        e.tensor_scalar(
                        Ta[:, 0, 0:ww],
                        Tb[:, ss, :, :].rearrange("p a b -> p (a b)")[:, 0:ww],
                        tpos3[:, 2:3], None, ALU.is_gt), inc=c_ve)
                    m_sgn3[(I, q)] = O("vector", lambda e, ss=wsl, ww=wd,
                        Ta=s3b, Tb=w3st: e.tensor_scalar(
                        Ta[:, 0, 0:ww],
                        Tb[:, ss, :, :].rearrange("p a b -> p (a b)")[:, 0:ww],
                        tneg3[:, 2:3], None, ALU.is_lt), inc=c_ve)
                    W("vector", c_ve, m_sgn3[(I, q)])
                    if I >= 2 and q == 0:
                        W("vector", c_pe, m_o3[(k, I - 2)])
                        ch_s3.wait("vector", slot)
                    m_q3add[(I, q)] = O("vector",
                        lambda e, a=lo, b=hi, so=slot, ww=wd, Ta=Q3s,
                        Tb=s3a, Tc=s3b:
                        e.tensor_sub(
                            Ta[:, so, a:b, :].rearrange("p a b -> p (a b)"),
                            Tb[:, 0, 0:ww], Tc[:, 0, 0:ww]), inc=c_ve)
                W("scalar", c_ve, m_q3add[(I, 3)])
                ch_s3.dma("scalar", slot, lambda e, II=I, so=slot, T=Q3s:
                    e.dma_start(
                    qd3[II].rearrange("p t i -> p (t i)"),
                    T[:, so, :, :].rearrange("p a b -> p (a b)")))
            else:
                for II2 in ([0, 1] if I == 0 else
                            [I + 1] if I + 1 < NS else []):
                    so2 = II2 % 2
                    if II2 >= 2:
                        W("sync", c_pe, m_o3[(k, II2 - 2)])
                    ch_q3.dma("sync", so2, lambda e, II=II2, so=so2, T=Q3s:
                        e.dma_start(
                        T[:, so, :, :].rearrange("p a b -> p (a b)"),
                        qd3[II].rearrange("p t i -> p (t i)")))
            slot3 = I % 3
            if I == 0:
                W("scalar", c_ve, ve_after_4a)
                W("scalar", c_act, act_after_4a)
                W("scalar", c_pe, pe_after_4a)
            if I >= 3:
                W("scalar", c_ve, m_oev[(k, I - 3)])
                ch_out.wait("scalar", slot3)
            ch_stw.wait("scalar", I % 2)
            ch_str.dma("scalar", slot3, lambda e, II=I, kk=k, T=sTs, s3=slot3:
                e.dma_start(
                T[:, s3, :], sTd[II][:, kk * CH:(kk + 1) * CH]))
            ch_xtb.dma("scalar", slot3, lambda e, II=I, kk=k, T=xts, s3=slot3:
                e.dma_start(
                T[:, s3, :],
                xtb[II * 128:(II + 1) * 128, kk * CH:(kk + 1) * CH]))
            if I >= 1:
                emit_silu4b(I - 1)
            if I >= 2:
                emit_comb4b(I - 2)
            if k == 0:
                W("tensor", c_ve, m_q3add[(I, 3)])
            else:
                ch_q3.wait("tensor", slot)
            if I == 0:
                W("tensor", c_pe, pe_after_4a)
                W("tensor", c_act, act_after_4a)
                W("tensor", c_ve, ve_after_4a)
            if I >= 2:
                W("tensor", c_act, m_g[(k, I - 2)])
            for t in range(NH):
                for n in range(2):
                    last = (t == NH - 1 and n == 1)
                    mm_ = O("tensor", lambda e, tt=t, nn=n, so=slot, Ta=o3ps,
                            Tb=Q3s: e.matmul(
                        Ta[:, so, nn * 512:(nn + 1) * 512],
                        Tb[:, so, tt, :],
                        hT[:, tt, nn * 512:(nn + 1) * 512],
                        start=(tt == 0), stop=(tt == NH - 1)),
                        inc=c_pe if last else None)
                    if last:
                        m_o3[(k, I)] = mm_

        for _hh in (NS - 1,):
            emit_silu4b(_hh)
        for _hh in (NS - 2, NS - 1):
            emit_comb4b(_hh)


        ph4b.close()

    ch_out.wait_all("scalar")
    for c in (ch_s1, ch_s2, ch_s3, ch_q1, ch_q2, ch_q3, ch_w3, ch_wg):
        c.wait_all("sync")

    ph2.close()
    return emit_all(())


def kernel(L0_attn_x, L0_ffn_w1, L0_ffn_w2, L0_ffn_w3):
    x = np.ascontiguousarray(np.asarray(L0_attn_x, dtype=np.float32))
    w1 = np.asarray(L0_ffn_w1, dtype=np.float32)
    w2 = np.asarray(L0_ffn_w2, dtype=np.float32)
    w3 = np.asarray(L0_ffn_w3, dtype=np.float32)

    w1t = np.ascontiguousarray(w1.reshape(NH, 128, 16, 128).transpose(0, 3, 2, 1))
    w2t = np.ascontiguousarray(w2.reshape(NH, 128, 16, 128).transpose(0, 3, 2, 1))
    w3t = np.ascontiguousarray(w3.reshape(NS, 128, NH, 128).transpose(0, 3, 2, 1))

    in_maps = []
    for core in range(8):
        b, p = divmod(core, 2)
        own = x[b, p * S_OWN:(p + 1) * S_OWN]
        oth = x[b, (1 - p) * S_OWN:(2 - p) * S_OWN]
        xbc = np.ascontiguousarray(np.concatenate([own, oth], axis=0))
        xtbc = np.ascontiguousarray(own.T)
        in_maps.append({"xb": xbc, "xtb": xtbc, "w1t": w1t, "w2t": w2t,
                        "w3t": w3t})

    nc = build_program()
    res = run_bass_kernel_spmd(nc, in_maps, core_ids=list(range(8)))

    out = np.empty((4, S_ALL, D), dtype=np.float32)
    for core in range(8):
        b, p = divmod(core, 2)
        out[b, p * S_OWN:(p + 1) * S_OWN, :] = res.results[core]["outT"].T
    return out
